# version 01
# baseline (speedup 1.0000x reference)
import numpy as np
import ml_dtypes
import concourse.bass as bass
import concourse.mybir as mybir
from concourse.bass_utils import run_bass_kernel_spmd
from contextlib import ExitStack

F32 = mybir.dt.float32
BF16 = mybir.dt.bfloat16
I32 = mybir.dt.int32
AF = mybir.ActivationFunctionType
ALU = mybir.AluOpType
AX = mybir.AxisListType

L = 2048
D = 1024
H = 16
DK = 64
U = 41
E = 8
HD = 2048
LC = 16
KC = 8
TF = 512
NTF = L // TF
NHE = 2 * E
EPS_LN = 1e-5
EPS_RMS = float(np.finfo(np.float32).eps)


def build_nc(stop_after="P7", debug=()):
    nc = bass.Bass("TRN2", target_bir_lowering=False)
    es = ExitStack()

    x_d = nc.dram_tensor("x", [L, D], F32, kind="ExternalInput").ap()
    wq_d = nc.dram_tensor("W_q", [D, D], F32, kind="ExternalInput").ap()
    bq_d = nc.dram_tensor("b_q", [D], F32, kind="ExternalInput").ap()
    wk_d = nc.dram_tensor("W_k", [D, D], F32, kind="ExternalInput").ap()
    bk_d = nc.dram_tensor("b_k", [D], F32, kind="ExternalInput").ap()
    wv_d = nc.dram_tensor("W_v", [D, D], F32, kind="ExternalInput").ap()
    bv_d = nc.dram_tensor("b_v", [D], F32, kind="ExternalInput").ap()
    wo_d = nc.dram_tensor("W_o", [D, D], F32, kind="ExternalInput").ap()
    bo_d = nc.dram_tensor("b_o", [D], F32, kind="ExternalInput").ap()
    ksi_d = nc.dram_tensor("ksi", [U], F32, kind="ExternalInput").ap()
    gw_d = nc.dram_tensor("gate_w", [E, D], F32, kind="ExternalInput").ap()
    gb_d = nc.dram_tensor("gate_b", [E], F32, kind="ExternalInput").ap()
    rw_d = nc.dram_tensor("rms_w", [E], F32, kind="ExternalInput").ap()
    w1_d = nc.dram_tensor("w1", [E, D, HD], F32, kind="ExternalInput").ap()
    b1_d = nc.dram_tensor("b1", [E, HD], F32, kind="ExternalInput").ap()
    w2_d = nc.dram_tensor("w2", [E, HD, D], F32, kind="ExternalInput").ap()
    b2_d = nc.dram_tensor("b2", [E, D], F32, kind="ExternalInput").ap()
    out_d = nc.dram_tensor("out", [L, D], F32, kind="ExternalOutput").ap()

    dbg = {}

    def mkdbg(name, shape, dtype=F32):
        if name in debug:
            dbg[name] = nc.dram_tensor("dbg_" + name, shape, dtype,
                                       kind="ExternalOutput").ap()

    mkdbg("xtm", [128, LC, D], BF16)
    mkdbg("gst", [128, LC, U], BF16)
    mkdbg("idxbc", [128, U])
    mkdbg("idxf", [1, U])
    mkdbg("lcols", [128, LC])
    mkdbg("xfm", [128, KC, L], BF16)
    mkdbg("xs", [128, KC, U], BF16)
    mkdbg("qtm", [128, LC, D], BF16)
    mkdbg("kfm", [128, KC, L], BF16)
    mkdbg("vtm", [128, LC, D], BF16)
    mkdbg("ks", [128, E, U], BF16)
    mkdbg("meanv", [1, D])
    mkdbg("mpack", [128, H, LC])
    mkdbg("qf2", [128, LC, 128], BF16)
    mkdbg("mhm", [16, L])
    mkdbg("ks2", [128, E, U], BF16)
    mkdbg("tau", [16, 48])
    mkdbg("rstar", [128, H, LC])
    mkdbg("ctx", [128, KC, L], BF16)
    mkdbg("qst7", [128, U], BF16)
    mkdbg("pt7", [128, 2, LC, U], BF16)
    mkdbg("rc7", [128, 1])
    mkdbg("da7", [128, DK], BF16)
    mkdbg("gaug7", [128, L], BF16)
    mkdbg("gta7", [128, 2, LC, 42], BF16)
    mkdbg("e42", [1, 42], BF16)
    mkdbg("mvb", [1, D], BF16)
    mkdbg("duraw", [128, DK])
    mkdbg("x1b", [128, KC, L], BF16)
    mkdbg("ng", [128, LC])
    mkdbg("logit", [128, LC, E])
    mkdbg("afin", [128, KC, L])
    for _bn in range(1, 6):
        mkdbg(f"borow{_bn}", [1, D], BF16)

    sems, C = {}, {}
    for name in ["sD", "gD", "sT", "sV", "sS", "sG"]:
        sems[name] = es.enter_context(nc.semaphore(name))
        C[name] = 0

    def tick(name, instr, n=1):
        instr.then_inc(sems[name], n)
        C[name] += n
        return C[name]

    def wait(eng, name, val=None):
        v = C[name] if val is None else val
        if v > 0:
            eng.wait_ge(sems[name], v)

    def vsp(e):
        e.tensor_copy(SPC[:, 256:512], SPC[:, 0:256])

    def barrier(eng, snap=None):
        src = C if snap is None else snap
        for name in sems:
            if src[name] > 0:
                eng.wait_ge(sems[name], src[name])

    def sb_in(stack, name, shape, dtype, side=None):
        return stack.enter_context(nc.sbuf_tensor(name, shape, dtype, side=side))

    def psum_in(stack, name, shape, dtype=F32):
        return stack.enter_context(nc.psum_tensor(name, shape, dtype))

    ident_f = sb_in(es, "ident_f", [128, 128], F32)
    ident_b = sb_in(es, "ident_b", [128, 128], BF16)
    tri_b = sb_in(es, "tri_b", [128, 128], BF16)
    iota_c = sb_in(es, "iota_c", [128, 1], F32)
    lcols = sb_in(es, "lcols", [128, LC], F32)
    ones_row_b = sb_in(es, "ones_row_b", [1, 128], BF16)
    ones_row_f = sb_in(es, "ones_row_f", [1, 128], F32)
    ones_col_b = sb_in(es, "ones_col_b", [128, 1], BF16)
    u_iota_bc = sb_in(es, "u_iota_bc", [128, U], F32)
    u42_bc = sb_in(es, "u42_bc", [128, 42], F32)
    ic42 = sb_in(es, "ic42", [128, 1], F32)
    e42_row = sb_in(es, "e42_row", [1, 42], BF16)
    zeros16 = sb_in(es, "zeros16", [128, LC], F32)
    bo_row = sb_in(es, "bo_row", [1, D], BF16)
    bk_cols = sb_in(es, "bk_cols", [128, KC], F32)
    gate_wT = sb_in(es, "gate_wT", [128, KC, E], BF16)
    gate_b_bc = sb_in(es, "gate_b_bc", [128, E], F32)
    rms_w_bc = sb_in(es, "rms_w_bc", [128, E], F32)
    b1c = sb_in(es, "b1c", [128, 16, E], F32)
    sb2c = sb_in(es, "sb2c", [128, KC], F32)
    idx_f = sb_in(es, "idx_f", [1, U], F32)
    idx_bc = sb_in(es, "idx_bc", [128, U], F32)
    i42m = sb_in(es, "i42m", [128, 1], F32)
    NG = sb_in(es, "NG", [128, LC], F32)
    eps_col = sb_in(es, "eps_col", [128, 1], F32)
    SPC = sb_in(es, "SPC", [128, 512], F32)

    esA = ExitStack()
    QTM = sb_in(esA, "QTM", [128, LC, D], BF16)
    KFM = sb_in(esA, "KFM", [128, KC, L], BF16)
    VTM = sb_in(esA, "VTM", [128, LC, D], BF16)
    mv_row_f = sb_in(esA, "mv_row_f", [1, D], F32)
    mv_row_b = sb_in(esA, "mv_row_b", [1, D], BF16)
    esB = ExitStack()
    KS = sb_in(esB, "KS", [128, E, U], BF16)
    esC = ExitStack()
    XS = sb_in(esC, "XS", [128, KC, U], BF16)
    GsT = sb_in(esC, "GsT", [128, LC, U], BF16)
    XFM = sb_in(esC, "XFM", [128, KC, L], BF16)
    bq_row = sb_in(esC, "bq_row", [1, D], BF16)
    bv_row = sb_in(esC, "bv_row", [1, D], BF16)

    esW = ExitStack()

    blk = es.enter_context(nc.Block())

    p0 = ExitStack()
    irow128 = sb_in(p0, "irow128", [128, 128], F32)
    stage2_f = sb_in(p0, "stage2_f", [E, D], F32)
    b1s = sb_in(p0, "b1s", [E, HD], F32)
    e42f = sb_in(p0, "e42f", [1, 42], F32)
    b2s = sb_in(p0, "b2s", [E, D], F32)
    bq_st = sb_in(p0, "bq_st", [1, D], F32)
    bk_st = sb_in(p0, "bk_st", [1, D], F32)
    bv_st = sb_in(p0, "bv_st", [1, D], F32)
    bo_st = sb_in(p0, "bo_st", [1, D], F32)
    gb_st = sb_in(p0, "gb_st", [1, E], F32)
    rw_st = sb_in(p0, "rw_st", [1, E], F32)
    ps_bk = psum_in(p0, "ps_bk", [128, KC])
    ps_gw = psum_in(p0, "ps_gw", [128, KC, E])
    ps_b1 = psum_in(p0, "ps_b1", [128, 16, E])
    ps_b2 = psum_in(p0, "ps_b2", [128, KC, E])
    ps_bc = psum_in(p0, "ps_bc", [128, U + 2 * E])

    def p0_sync(e):
        tick("sD", e.dma_start(idx_f[:, :], ksi_d.rearrange("(a u) -> a u", a=1)), 16)
        tick("sD", e.dma_start(bq_st[:, :], bq_d.rearrange("(a u) -> a u", a=1)), 16)
        tick("sD", e.dma_start(bk_st[:, :], bk_d.rearrange("(a u) -> a u", a=1)), 16)
        tick("sD", e.dma_start(bv_st[:, :], bv_d.rearrange("(a u) -> a u", a=1)), 16)
        tick("sD", e.dma_start(bo_st[:, :], bo_d.rearrange("(a u) -> a u", a=1)), 16)
        tick("sD", e.dma_start(gb_st[:, :], gb_d.rearrange("(a u) -> a u", a=1)), 16)
        tick("sD", e.dma_start(rw_st[:, :], rw_d.rearrange("(a u) -> a u", a=1)), 16)
        tick("sD", e.dma_start(b2s[:, :], b2_d), 16)
        tick("sD", e.dma_start(stage2_f[0:E, 0:D], gw_d), 16)
        tick("sD", e.dma_start(b1s[:, :], b1_d), 16)

    def p0_gpsimd(e):
        kw = dict(allow_small_or_imprecise_dtypes=True)
        tick("sG", e.iota(iota_c[:, :], [[0, 1]], base=0, channel_multiplier=1, **kw))
        tick("sG", e.iota(lcols[:, :], [[128, LC]], base=0, channel_multiplier=1, **kw))
        tick("sG", e.iota(irow128[:, :], [[1, 128]], base=0, channel_multiplier=0, **kw))
        tick("sG", e.iota(u_iota_bc[:, :], [[1, U]], base=1, channel_multiplier=0, **kw))
        tick("sG", e.iota(u42_bc[:, :], [[1, 42]], base=1, channel_multiplier=0, **kw))
        tick("sG", e.memset(u42_bc[:, 41:42], 0.0))
        tick("sG", e.iota(ic42[:, :], [[0, 1]], base=1, channel_multiplier=1, **kw))
        tick("sG", e.memset(zeros16[:, :], 0.0))
        tick("sG", e.memset(ones_row_b[:, :], 1.0))
        tick("sG", e.memset(ones_row_f[:, :], 1.0))
        tick("sG", e.memset(ones_col_b[:, :], 1.0))

        tick("sG", e.memset(eps_col[:, :], EPS_LN))
        tick("sG", e.memset(SPC[:, :], 0.0))

    def p0_vector(e):
        wait(e, "sG")
        tick("sV", e.tensor_scalar(ident_f[:, :], irow128[:, :], iota_c[:, :],
                                   None, ALU.is_equal))
        vsp(e)
        tick("sV", e.tensor_copy(ident_b[:, :], ident_f[:, :]))
        tick("sV", e.tensor_scalar(tri_b[:, :], irow128[:, :], iota_c[:, :],
                                   None, ALU.is_gt))
        tick("sV", e.tensor_single_scalar(ic42[64:128, :], ic42[64:128, :], 64.0,
                                          ALU.subtract))
        vsp(e)
        tick("sV", e.tensor_single_scalar(i42m[:, :], ic42[:, :], 42.0,
                                          ALU.is_equal))
        vsp(e)
        tick("sV", e.scalar_tensor_tensor(ic42[:, :], i42m[:, :], -42.0,
                                          ic42[:, :], ALU.mult, ALU.add))
        tick("sV", e.tensor_single_scalar(e42f[:, :], irow128[0:1, 0:42], 41.0,
                                          ALU.is_equal))
        vsp(e)
        tick("sV", e.tensor_copy(e42_row[:, :], e42f[:, :]))
        wait(e, "sD")
        tick("sV", e.tensor_single_scalar(idx_f[:, :], idx_f[:, :], float(L - 1),
                                          ALU.min))
        tick("sV", e.engine_nop())
        tick("sV", e.tensor_copy(bq_row[:, :], bq_st[:, :]))
        tick("sV", e.tensor_copy(bv_row[:, :], bv_st[:, :]))
        tick("sV", e.tensor_copy(bo_row[:, :], bo_st[:, :]))

    def p0_tensor(e):
        wait(e, "sV")
        for m in range(KC):
            tick("sT", e.transpose(ps_bk[:, m:m + 1],
                                   bk_st[0:1, m * 128:(m + 1) * 128],
                                   ident_f[0:1, 0:1]))
        for k in range(KC):
            tick("sT", e.transpose(ps_gw[:, k, :], stage2_f[:, k * 128:(k + 1) * 128],
                                   ident_f[0:E, 0:E]))
        for hc in range(16):
            tick("sT", e.transpose(ps_b1[:, hc, :], b1s[:, hc * 128:(hc + 1) * 128],
                                   ident_f[0:E, 0:E]))
        for m in range(KC):
            tick("sT", e.transpose(ps_b2[:, m, :],
                                   b2s[:, m * 128:(m + 1) * 128],
                                   ident_f[0:E, 0:E]))
        tick("sT", e.matmul(ps_bc[:, 0:U], ones_row_f[:, :], idx_f[0:1, :],
                            start=True, stop=True))
        tick("sT", e.matmul(ps_bc[:, U:U + E], ones_row_f[:, :], gb_st[0:1, :],
                            start=True, stop=True))
        tick("sT", e.matmul(ps_bc[:, U + E:U + 2 * E], ones_row_f[:, :],
                            rw_st[0:1, :], start=True, stop=True))

    def p0_vector2(e):
        wait(e, "sT")
        tick("sV", e.tensor_copy(bk_cols[:, :], ps_bk[:, :]))
        for k in range(KC):
            tick("sV", e.tensor_copy(gate_wT[:, k, :], ps_gw[:, k, :]))
        for hc in range(16):
            tick("sV", e.tensor_copy(b1c[:, hc, :], ps_b1[:, hc, :]))
        for m in range(KC):
            tick("sV", e.reduce_sum(sb2c[:, m:m + 1], ps_b2[:, m, :], axis=AX.X))
        tick("sV", e.tensor_copy(idx_bc[:, :], ps_bc[:, 0:U]))
        tick("sV", e.tensor_copy(gate_b_bc[:, :], ps_bc[:, U:U + E]))
        tick("sV", e.tensor_copy(rms_w_bc[:, :], ps_bc[:, U + E:U + 2 * E]))
        for c in range(LC):
            tick("sV", e.tensor_scalar(GsT[:, c, :], idx_bc[:, :],
                                       lcols[:, c:c + 1], None, ALU.is_equal))

    blk.sync(p0_sync)
    blk.gpsimd(p0_gpsimd)
    blk.vector(p0_vector)
    blk.tensor(p0_tensor)
    blk.vector(p0_vector2)
    p0.close()

    p1 = ExitStack()
    XTM = sb_in(p1, "XTM", [128, LC, D], BF16)
    XT1 = [sb_in(p1, f"XT1_{i}", [128, D], F32) for i in range(2)]
    ps_tr = [psum_in(p1, f"ps_tr{i}", [128, 4, 128], BF16) for i in range(2)]
    ps_xs = [psum_in(p1, f"ps_xs{i}", [128, U]) for i in range(2)]

    sT0p1 = C["sT"]
    sV0p1 = C["sV"]
    sD0p1 = C["sD"]

    def D_x1(c):
        return sD0p1 + 16 * (c + 1)

    def V_cast1(c):
        return sV0p1 + c + 1

    def T_tr1(ci):
        return sT0p1 + 4 * (ci + 1)

    def T_xs1(k):
        return sT0p1 + 128 + 16 * (k + 1)

    def V_tr1(ci):
        return sV0p1 + 16 + ci + 1

    def V_xs1(k):
        return sV0p1 + 48 + k + 1

    def p1_sync(e):
        barrier(e)
        for c in range(LC):
            if c >= 2:
                wait(e, "sV", V_cast1(c - 2))
            tick("sD", e.dma_start(
                XT1[c % 2][:, :],
                x_d.rearrange("(c p) n -> p c n", p=128)[:, c, :]), 16)

    def p1_tensor(e):
        wait(e, "sV", sV0p1)
        for ci in range(2 * LC):
            c, half = ci // 2, ci % 2
            wait(e, "sV", V_cast1(c))
            if ci >= 2:
                wait(e, "sV", V_tr1(ci - 2))
            for k in range(4):
                kk = half * 4 + k
                tick("sT", e.transpose(ps_tr[ci % 2][:, k, :],
                                       XTM[:, c, kk * 128:(kk + 1) * 128],
                                       ident_b[:, :]))
        wait(e, "sV", V_cast1(LC - 1))
        for k in range(KC):
            if k >= 2:
                wait(e, "sV", V_xs1(k - 2))
            for c in range(LC):
                tick("sT", e.matmul(ps_xs[k % 2][:, :],
                                    XTM[:, c, k * 128:(k + 1) * 128],
                                    GsT[:, c, :],
                                    start=(c == 0), stop=(c == LC - 1)))

    def p1_vector(e):
        for c in range(LC):
            wait(e, "sD", D_x1(c))
            tick("sV", e.tensor_copy(XTM[:, c, :], XT1[c % 2][:, :]))
        for ci in range(2 * LC):
            c, half = ci // 2, ci % 2
            wait(e, "sT", T_tr1(ci))
            tick("sV", e.tensor_copy(
                XFM[:, half * 4:(half + 1) * 4, c * 128:(c + 1) * 128],
                ps_tr[ci % 2][:, :, :]))
        for k in range(KC):
            wait(e, "sT", T_xs1(k))
            tick("sV", e.tensor_copy(XS[:, k, :], ps_xs[k % 2][:, :]))

    blk.sync(p1_sync)
    blk.tensor(p1_tensor)
    blk.vector(p1_vector)

    def dump(pairs):
        def f(e):
            barrier(e)
            for name, src in pairs:
                if name in dbg:
                    tick("sD", e.dma_start(dbg[name], src), 16)
            wait(e, "sD")
        blk.sync(f)

        def w(e):
            wait(e, "sD")
        blk.tensor(w)
        blk.vector(w)
        blk.scalar(w)
        blk.gpsimd(w)

    def finish(*stacks):
        def fin(e):
            barrier(e)
        blk.sync(fin)
        for s in stacks:
            s.close()
        es.close()
        return nc

    if debug:
        dump([("xtm", XTM[:, :, :]), ("xfm", XFM[:, :, :]), ("xs", XS[:, :, :]), ("gst", GsT[:, :, :]), ("idxbc", idx_bc[:, :]), ("lcols", lcols[:, :]), ("idxf", idx_f[:, :]), ("borow1", bo_row[:, :])])
    if stop_after == "P1":
        return finish(p1, esW, esC, esB, esA)

    p1.close()

    WB = [sb_in(esW, "WB0", [128, KC, D], BF16, side="right"),
          sb_in(esW, "WB1", [128, KC, D], BF16, side="right")]
    p2 = ExitStack()
    WL2 = sb_in(p2, "WL2", [128, 2, D], F32, side="right")
    ps_pp = [psum_in(p2, f"ps_pp{i}", [128, 512]) for i in range(4)]
    ps_ks = [psum_in(p2, f"ps_ks{i}", [128, U]) for i in range(2)]
    ps_mv = psum_in(p2, "ps_mv", [1, 2, 512])

    sT0p2 = C["sT"]
    sV0p2 = C["sV"]
    sS0p2 = C["sS"]
    gD0p2 = C["gD"]
    TQ_END = sT0p2 + 32
    TKS_END = sT0p2 + 72

    def T_grp2(g):
        if g < 64:
            return sT0p2 + g + 1
        return sT0p2 + 8 + g + 1

    def T_ks2(hp):
        return sT0p2 + 64 + hp + 1

    def T_mv2(half):
        return sT0p2 + 104 + half + 1

    def V_grp2(g):
        if g < 64:
            return sV0p2 + g + 1
        return sV0p2 + 8 + g + 1

    def V_ks2(hp):
        return sV0p2 + 64 + hp + 1

    W_SRCS = [wq_d, wk_d, wv_d, wo_d]

    def S2_cast(w, k):
        return sS0p2 + 8 * w + k + 1

    def p2_gpsimd(e):
        barrier(e)
        for w in range(4):
            if w == 2:
                wait(e, "sT", TQ_END)
            if w == 3:
                wait(e, "sT", TKS_END)
            for k in range(KC):
                ci = 8 * w + k
                if ci >= 2:
                    wait(e, "sS", S2_cast((ci - 2) // 8, (ci - 2) % 8))
                tick("gD", e.dma_start(
                    WL2[:, ci % 2, :],
                    W_SRCS[w].rearrange("(k p) n -> p k n", p=128)[:, k, :]), 16)

    def p2_scalar(e):
        for w in range(4):
            for k in range(KC):
                ci = 8 * w + k
                wait(e, "gD", gD0p2 + 16 * (ci + 1))
                tick("sS", e.copy(WB[w % 2][:, k, :], WL2[:, ci % 2, :]))

    S_wq = sS0p2 + 8
    S_wk = sS0p2 + 16
    S_wv = sS0p2 + 24
    S_wo = sS0p2 + 32

    def p2_tensor(e):
        wait(e, "sS", S_wq)
        g = 0
        for c in range(LC):
            for half in range(2):
                if g >= 4:
                    wait(e, "sV", V_grp2(g - 4))
                for k in range(KC):
                    e.matmul(ps_pp[g % 4][:, :],
                             XFM[:, k, c * 128:(c + 1) * 128],
                             WB[0][:, k, half * 512:(half + 1) * 512],
                             start=(k == 0), stop=False)
                tick("sT", e.matmul(ps_pp[g % 4][:, :], ones_row_b[:, :],
                                    bq_row[0:1, half * 512:(half + 1) * 512],
                                    start=False, stop=True))
                g += 1
        wait(e, "sS", S_wk)
        for m in range(KC):
            for n in range(4):
                wait(e, "sV", V_grp2(g - 4))
                for k in range(KC):
                    instr = e.matmul(ps_pp[g % 4][:, :],
                                     WB[1][:, k, m * 128:(m + 1) * 128],
                                     XFM[:, k, n * 512:(n + 1) * 512],
                                     start=(k == 0), stop=(k == KC - 1))
                    if k == KC - 1:
                        tick("sT", instr)
                g += 1
        for hp in range(E):
            if hp >= 2:
                wait(e, "sV", V_ks2(hp - 2))
            for h01 in range(2):
                for k in range(KC):
                    instr = e.matmul(ps_ks[hp % 2][h01 * 64:(h01 + 1) * 64, :],
                                     WB[1][:, k, (2 * hp + h01) * 64:(2 * hp + h01 + 1) * 64],
                                     XS[:, k, :],
                                     start=(k == 0), stop=(k == KC - 1),
                                     tile_position=(0, h01 * 64))
                    if h01 == 1 and k == KC - 1:
                        tick("sT", instr)
        wait(e, "sS", S_wv)
        for c in range(LC):
            for half in range(2):
                wait(e, "sV", V_grp2(g - 4))
                for k in range(KC):
                    e.matmul(ps_pp[g % 4][:, :],
                             XFM[:, k, c * 128:(c + 1) * 128],
                             WB[0][:, k, half * 512:(half + 1) * 512],
                             start=(k == 0), stop=False)
                tick("sT", e.matmul(ps_pp[g % 4][:, :], ones_row_b[:, :],
                                    bv_row[0:1, half * 512:(half + 1) * 512],
                                    start=False, stop=True))
                g += 1
        wait(e, "sV", V_grp2(95))
        for half in range(2):
            for c in range(LC):
                instr = e.matmul(ps_mv[0:1, half, :], ones_col_b[:, :],
                                 VTM[:, c, half * 512:(half + 1) * 512],
                                 start=(c == 0), stop=(c == LC - 1))
                if c == LC - 1:
                    tick("sT", instr)

    def p2_vector(e):
        g = 0
        for c in range(LC):
            for half in range(2):
                wait(e, "sT", T_grp2(g))
                tick("sV", e.tensor_copy(QTM[:, c, half * 512:(half + 1) * 512],
                                         ps_pp[g % 4][:, :]))
                g += 1
        for m in range(KC):
            for n in range(4):
                wait(e, "sT", T_grp2(g))
                tick("sV", e.tensor_scalar(KFM[:, m, n * 512:(n + 1) * 512],
                                           ps_pp[g % 4][:, :], bk_cols[:, m:m + 1],
                                           None, ALU.add))
                g += 1
        for hp in range(E):
            wait(e, "sT", T_ks2(hp))
            tick("sV", e.tensor_scalar(KS[:, hp, :], ps_ks[hp % 2][:, :],
                                       bk_cols[:, hp:hp + 1], None, ALU.add))
        for c in range(LC):
            for half in range(2):
                wait(e, "sT", T_grp2(g))
                tick("sV", e.tensor_copy(VTM[:, c, half * 512:(half + 1) * 512],
                                         ps_pp[g % 4][:, :]))
                g += 1
        for half in range(2):
            wait(e, "sT", T_mv2(half))
            tick("sV", e.tensor_scalar(mv_row_f[0:1, half * 512:(half + 1) * 512],
                                       ps_mv[0:1, half, :], 1.0 / L, None,
                                       ALU.mult))
        tick("sV", e.tensor_copy(mv_row_b[:, :], mv_row_f[:, :]))

    blk.gpsimd(p2_gpsimd)
    blk.scalar(p2_scalar)
    blk.tensor(p2_tensor)
    blk.vector(p2_vector)
    p2.close()

    if debug:
        dump([("qtm", QTM[:, :, :]), ("kfm", KFM[:, :, :]),
              ("vtm", VTM[:, :, :]), ("ks", KS[:, :, :]),
              ("meanv", mv_row_f[:, :]), ("borow2", bo_row[:, :])])
    if stop_after == "P2":
        return finish(esW, esC, esB, esA)

    esC.close()
    esW.close()

    esR1 = ExitStack()
    CTX = sb_in(esR1, "CTX", [128, KC, L], BF16, side="right")
    esR2 = ExitStack()
    MPACK = sb_in(esR2, "MPACK", [128, H, LC], F32, side="right")
    TAUBC = sb_in(esR2, "TAUBC", [128, H], F32, side="right")
    SCNA = sb_in(esR2, "SCNA", [128, H, LC], F32, side="right")
    p3 = ExitStack()
    QF2 = [sb_in(p3, f"QF2_{i}", [128, LC, 128], BF16) for i in range(2)]
    MXA = sb_in(p3, "MXA", [128, H, LC], F32)
    NSA = sb_in(p3, "NSA", [128, H, LC], F32)
    MHM = sb_in(p3, "MHM", [16, L], F32)
    MMUT = sb_in(p3, "MMUT", [16, L], F32)
    MAXV = sb_in(p3, "MAXV", [16, 48], F32)
    T01 = sb_in(p3, "T01", [128, 2, 128], F32)
    TAUR = sb_in(p3, "TAUR", [1, 16], F32)
    SCN = sb_in(p3, "SCN", [128, 2, LC], F32)
    TOTB = sb_in(p3, "TOTB", [128, H], BF16)
    OFFS = sb_in(p3, "OFFS", [128, H], F32)
    MRED = sb_in(p3, "MRED", [128, 2, 8], F32)

    p3a = ExitStack()
    ps_qf = [psum_in(p3a, f"ps_qf{i}", [128, 4, 128], BF16) for i in range(2)]
    ps_qk = [psum_in(p3a, f"ps_qk{i}", [128, 8, U]) for i in range(4)]

    sT0p3 = C["sT"]
    sV0p3 = C["sV"]

    def T_qf3(gi):
        return sT0p3 + 8 * (gi // 4) + (gi % 4) + 1

    def T_qk3(qi):
        return sT0p3 + 8 * (qi // 4) + 4 + (qi % 4) + 1

    def V_qf3(gi):
        return sV0p3 + 8 * (gi // 4) + (gi % 4) + 1

    def V_mp3(qi):
        return sV0p3 + 8 * (qi // 4) + 4 + (qi % 4) + 1

    P3_SERIAL = True

    def p3_tensor(e):
        barrier(e)
        gi = 0
        qi = 0
        for hp in range(E):
            for c4 in range(4):
                if P3_SERIAL and (gi > 0 or hp > 0):
                    wait(e, "sV", V_qf3(gi - 1) if gi > 0 else 0)
                    if qi > 0:
                        wait(e, "sV", V_mp3(qi - 1))
                elif gi >= 2:
                    wait(e, "sV", V_qf3(gi - 2))
                for k in range(4):
                    c = c4 * 4 + k
                    instr = e.transpose(ps_qf[gi % 2][:, k, :],
                                        QTM[:, c, hp * 128:(hp + 1) * 128],
                                        ident_b[:, :])
                    if k == 3:
                        tick("sT", instr)
                gi += 1
            wait(e, "sV", V_qf3(4 * hp + 3))
            for h01 in range(2):
                for chalf in range(2):
                    if P3_SERIAL and qi > 0:
                        wait(e, "sV", V_mp3(qi - 1))
                    elif qi >= 4:
                        wait(e, "sV", V_mp3(qi - 4))
                    for c8 in range(8):
                        c = chalf * 8 + c8
                        instr = e.matmul(
                            ps_qk[qi % 4][:, c8, :],
                            QF2[hp % 2][h01 * 64:(h01 + 1) * 64, c, :],
                            KS[h01 * 64:(h01 + 1) * 64, hp, :],
                            start=True, stop=True,
                            tile_position=(h01 * 64, 0))
                        if c8 == 7:
                            tick("sT", instr)
                    qi += 1

    def p3_vector(e):
        gi = 0
        qi = 0
        for hp in range(E):
            for c4 in range(4):
                wait(e, "sT", T_qf3(gi))
                tick("sV", e.tensor_copy(QF2[hp % 2][:, c4 * 4:(c4 + 1) * 4, :],
                                         ps_qf[gi % 2][:, :, :]))
                gi += 1
            for h01 in range(2):
                h = 2 * hp + h01
                for chalf in range(2):
                    wait(e, "sT", T_qk3(qi))
                    e.reduce_max(MXA[:, h, chalf * 8:(chalf + 1) * 8],
                                 ps_qk[qi % 4][:, :, :], axis=AX.X)
                    tick("sV", e.reduce_sum(NSA[:, h, chalf * 8:(chalf + 1) * 8],
                                            ps_qk[qi % 4][:, :, :],
                                            axis=AX.X, negate=True))
                    qi += 1
        tick("sV", e.scalar_tensor_tensor(MPACK[:, :, :], NSA[:, :, :], 1.0 / U,
                                          MXA[:, :, :], ALU.mult, ALU.add))

    blk.tensor(p3_tensor)
    blk.vector(p3_vector)
    p3a.close()
    ps_mt = psum_in(p3, "ps_mt", [128, 128])
    ps_off = psum_in(p3, "ps_off", [128, H])

    def mk_t(fn):
        blk.tensor(fn)

    def mk_v(fn):
        blk.vector(fn)

    def mk_g(fn):
        blk.gpsimd(fn)

    def t_half(half):
        def f(e):
            wait(e, "sV")
            m2d = MPACK[:, :, :].rearrange("p h c -> p (h c)")
            tick("sT", e.transpose(ps_mt[:, :], m2d[:, half * 128:(half + 1) * 128],
                                   ident_f[:, :]))
        return f

    def v_half(half):
        def f(e):
            wait(e, "sT")
            tick("sV", e.tensor_copy(T01[:, half, :], ps_mt[:, :]))
        return f

    mk_t(t_half(0))
    mk_v(v_half(0))
    mk_t(t_half(1))
    mk_v(v_half(1))

    def g_flat(e):
        wait(e, "sV")
        tick("gD", e.dma_start(MHM[0:8, :], T01[:, 0, :]), 16)
        tick("gD", e.dma_start(MHM[8:16, :], T01[:, 1, :]), 16)
    mk_g(g_flat)

    def v_topk(e):
        wait(e, "gD")
        tick("sV", e.tensor_copy(MMUT[:, :], MHM[:, :]))
        for r in range(5):
            sl = slice(r * 8, r * 8 + 8)
            tick("sV", e.max(MAXV[:, sl], MMUT[:, :]))
            vsp(e)
            tick("sV", e.match_replace(MMUT[:, :], MAXV[:, sl], MMUT[:, :], -1e30))
        tick("sV", e.max(MAXV[:, 40:48], MMUT[:, :]))
    mk_v(v_topk)

    def t_taur(e):
        wait(e, "sV")
        tick("sT", e.transpose(ps_mt[0:1, 0:16], MAXV[:, 40:41],
                               ident_f[0:16, 0:16]))
    mk_t(t_taur)

    def v_taur(e):
        wait(e, "sT")
        tick("sV", e.tensor_copy(TAUR[0:1, :], ps_mt[0:1, 0:16]))
    mk_v(v_taur)

    def t_taubc(e):
        wait(e, "sV")
        tick("sT", e.matmul(ps_mt[:, 0:16], ones_row_f[:, :], TAUR[0:1, :],
                            start=True, stop=True))
    mk_t(t_taubc)

    def v_taubc(e):
        wait(e, "sT")
        tick("sV", e.tensor_copy(TAUBC[:, :], ps_mt[:, 0:16]))
    mk_v(v_taubc)

    MSKA = sb_in(p3, "MSKA", [128, H, LC], F32)

    def v_rank_a2(e):
        vsp(e)
        for h in range(H):
            tick("sV", e.tensor_scalar(MSKA[:, h, :], MPACK[:, h, :],
                                       TAUBC[:, h:h + 1], None, ALU.is_ge))
        vsp(e)
        for h in range(H):
            tick("sV", e.tensor_tensor_scan(SCNA[:, h, :], MSKA[:, h, :],
                                            zeros16[:, :], 0.0, ALU.add, ALU.add))
        vsp(e)
        for h in range(H):
            tick("sV", e.tensor_copy(TOTB[:, h:h + 1], SCNA[:, h, LC - 1:LC]))
    mk_v(v_rank_a2)

    def t_offs(e):
        wait(e, "sV")
        for h in range(H):
            instr = e.matmul(ps_off[:, h:h + 1], tri_b[:, :], TOTB[:, h:h + 1],
                             start=True, stop=True)
            if h == H - 1:
                tick("sT", instr)
    mk_t(t_offs)

    def v_rank_b(e):
        wait(e, "sT")
        tick("sV", e.tensor_copy(OFFS[:, :], ps_off[:, :]))
        vsp(e)
        for h in range(H):
            tick("sV", e.tensor_scalar(SCNA[:, h, :], SCNA[:, h, :],
                                       OFFS[:, h:h + 1], None, ALU.add))
        vsp(e)
        for h in range(H):
            tick("sV", e.tensor_mul(SCNA[:, h, :], SCNA[:, h, :], MSKA[:, h, :]))
    mk_v(v_rank_b)

    if debug:
        dump([("mpack", MPACK[:, :, :]), ("tau", MAXV[:, :]),
              ("rstar", SCNA[:, :, :]), ("qf2", QF2[1][:, :, :]), ("mhm", MHM[:, :]),
              ("ks2", KS[:, :, :]), ("borow3", bo_row[:, :])])
    if stop_after == "P3":
        return finish(p3, esR2, esR1, esB, esA)

    p3.close()
    esB.close()

    p4 = ExitStack()
    GAUG = [sb_in(p4, f"GAUG{i}", [128, L], BF16) for i in range(2)]
    DUDBG = sb_in(p4, "DUDBG", [128, DK], F32)
    GTA4 = [sb_in(p4, f"GTA4_{i}", [128, 2, LC, 42], BF16) for i in range(2)]
    QST = [sb_in(p4, f"QST{i}", [128, U], BF16) for i in range(2)]
    PT = [sb_in(p4, f"PT{i}", [128, 2, LC, U], BF16) for i in range(2)]
    DA = [sb_in(p4, f"DA{i}", [128, DK], BF16) for i in range(2)]
    RC4 = [sb_in(p4, f"RC4_{i}", [128, 1], F32) for i in range(2)]

    ps_ga = [psum_in(p4, f"ps_ga{i}", [128, 4, 128], BF16) for i in range(2)]
    ps_qs = psum_in(p4, "ps_qs", [128, U])
    ps_st = [psum_in(p4, f"ps_st{i}", [128, 8, U]) for i in range(2)]
    ps_misc = psum_in(p4, "ps_misc", [128, 128])
    ps_cx = [psum_in(p4, f"ps_cx{i}", [128, 512]) for i in range(2)]

    T0p4 = C["sT"]
    V0p4 = C["sV"]
    S0p4 = C["sS"]
    TPP = 19
    VPP = 18
    SPP = 4

    def p4_gpsimd(e):
        barrier(e)
        tick("sG", e.memset(RC4[0][:, :], 1.0))
        tick("sG", e.memset(RC4[1][:, :], 1.0))

    def T4_ga(hp, i):
        return T0p4 + TPP * hp + i + 1

    def T4_qs(hp):
        return T0p4 + TPP * hp + 9

    def T4_st(hp, s):
        return T0p4 + TPP * hp + 10 + s

    def T4_rs(hp):
        return T0p4 + TPP * hp + 14

    def T4_du(hp):
        return T0p4 + TPP * hp + 15

    def T4_cx(hp, n):
        return T0p4 + TPP * hp + 16 + n

    def V4_gt(hp):
        return V0p4 + VPP * hp + 1

    def V4_ga(hp, i):
        return V0p4 + VPP * hp + 2 + i

    def V4_qs(hp):
        return V0p4 + VPP * hp + 10

    def V4_rcp(hp):
        return V0p4 + VPP * hp + 12

    def V4_da(hp):
        return V0p4 + VPP * hp + 14

    def V4_cx(hp, n):
        return V0p4 + VPP * hp + 15 + n

    def S4_st(hp, s):
        return S0p4 + SPP * hp + s + 1

    def p4_tensor(e):
        barrier(e)
        for hp in range(E):
            par = hp % 2
            h0, h1 = 2 * hp, 2 * hp + 1
            wait(e, "sV", V4_gt(hp))
            for h01 in range(2):
                for nb in range(4):
                    i = h01 * 4 + nb
                    bi = 8 * hp + i
                    if bi >= 2:
                        wait(e, "sV", V4_ga((bi - 2) // 8, (bi - 2) % 8))
                    for k in range(4):
                        c = nb * 4 + k
                        instr = e.transpose(
                            ps_ga[bi % 2][h01 * 64:h01 * 64 + 42, k, :],
                            GTA4[par][:, h01, c, :],
                            ident_b[:, :],
                            tile_position=(0, h01 * 64))
                        if k == 3:
                            tick("sT", instr)
            if hp >= 1:
                wait(e, "sV", V4_qs(hp - 1))
            for h01, h in ((0, h0), (1, h1)):
                for c in range(LC):
                    instr = e.matmul(ps_qs[h01 * 64:(h01 + 1) * 64, :],
                                     QTM[:, c, h * 64:(h + 1) * 64],
                                     GTA4[par][:, h01, c, 0:U],
                                     start=(c == 0), stop=(c == LC - 1),
                                     tile_position=(0, h01 * 64))
                    if h01 == 1 and c == LC - 1:
                        tick("sT", instr)
            wait(e, "sV", V4_qs(hp))
            for h01 in range(2):
                for chalf in range(2):
                    s = 2 * h01 + chalf
                    sg = 4 * hp + s
                    if sg >= 2:
                        wait(e, "sS", S4_st((sg - 2) // 4, (sg - 2) % 4))
                    for c8 in range(8):
                        c = chalf * 8 + c8
                        instr = e.matmul(
                            ps_st[sg % 2][:, c8, :],
                            KFM[h01 * 64:(h01 + 1) * 64, hp, c * 128:(c + 1) * 128],
                            QST[par][h01 * 64:(h01 + 1) * 64, :],
                            start=True, stop=True,
                            tile_position=(h01 * 64, 0))
                        if c8 == 7:
                            tick("sT", instr)
            wait(e, "sS", S4_st(hp, 3))
            if hp >= 1:
                wait(e, "sV", V4_da(hp - 1))
            for h01 in range(2):
                for c in range(LC):
                    instr = e.matmul(ps_misc[h01 * 64:h01 * 64 + U, 124:125],
                                     PT[par][:, h01, c, :],
                                     ones_col_b[:, :],
                                     start=(c == 0), stop=(c == LC - 1),
                                     tile_position=(0, h01 * 64))
                    if h01 == 1 and c == LC - 1:
                        tick("sT", instr)
            wait(e, "sV", V4_rcp(hp))
            for h01, h in ((0, h0), (1, h1)):
                e.matmul(ps_misc[h01 * 64:h01 * 64 + 42, 0:DK],
                         e42_row[:, :], mv_row_b[0:1, h * 64:(h + 1) * 64],
                         start=True, stop=False,
                         tile_position=(0, h01 * 64))
                for c in range(LC):
                    instr = e.matmul(ps_misc[h01 * 64:h01 * 64 + U, 0:DK],
                                     PT[par][:, h01, c, :],
                                     VTM[:, c, h * 64:(h + 1) * 64],
                                     start=False, stop=(c == LC - 1),
                                     tile_position=(0, h01 * 64))
                    if h01 == 1 and c == LC - 1:
                        tick("sT", instr)
            wait(e, "sV", V4_da(hp))
            wait(e, "sV", V4_ga(hp, 7))
            for n in range(4):
                b = 4 * hp + n
                if b >= 2:
                    wait(e, "sV", V4_cx((b - 2) // 4, (b - 2) % 4))
                e.matmul(ps_cx[n % 2][0:64, :], DA[par][0:42, :],
                         GAUG[par][0:42, n * 512:(n + 1) * 512],
                         start=True, stop=True)
                tick("sT", e.matmul(ps_cx[n % 2][64:128, :], DA[par][64:106, :],
                                    GAUG[par][64:106, n * 512:(n + 1) * 512],
                                    start=True, stop=True,
                                    tile_position=(64, 64)))

    def p4_scalar(e):
        for hp in range(E):
            par = hp % 2
            for h01 in range(2):
                for chalf in range(2):
                    s = 2 * h01 + chalf
                    wait(e, "sT", T4_st(hp, s))
                    tick("sS", e.activation(
                        PT[par][:, h01, chalf * 8:(chalf + 1) * 8, :],
                        ps_st[(4 * hp + s) % 2][:, :, :], AF.Exp,
                        bias=0.0, scale=1.0 / 8.0))

    def p4_vector(e):
        for hp in range(E):
            par = hp % 2
            h0, h1 = 2 * hp, 2 * hp + 1
            if hp >= 2:
                wait(e, "sT", T4_qs(hp - 2))
            for h01, h in ((0, h0), (1, h1)):
                for c in range(LC):
                    instr = e.tensor_scalar(GTA4[par][:, h01, c, :],
                                            u42_bc[:, :],
                                            SCNA[:, h, c:c + 1], None,
                                            ALU.is_equal)
                    if h01 == 1 and c == LC - 1:
                        tick("sV", instr)
            for h01 in range(2):
                for nb in range(4):
                    i = h01 * 4 + nb
                    bi = 8 * hp + i
                    wait(e, "sT", T4_ga(hp, i))
                    tick("sV", e.tensor_copy(
                        GAUG[par][h01 * 64:h01 * 64 + 42,
                                  nb * 512:(nb + 1) * 512],
                        ps_ga[bi % 2][h01 * 64:h01 * 64 + 42, :, :]))
            wait(e, "sT", T4_qs(hp))
            tick("sV", e.tensor_copy(QST[par][:, :], ps_qs[:, :]))
            wait(e, "sT", T4_rs(hp))
            tick("sV", e.reciprocal(RC4[par][0:U, :], ps_misc[0:U, 124:125]))
            tick("sV", e.reciprocal(RC4[par][64:64 + U, :],
                                    ps_misc[64:64 + U, 124:125]))
            vsp(e)
            wait(e, "sT", T4_du(hp))
            if hp == 7:
                e.tensor_copy(DUDBG[0:42, :], ps_misc[0:42, 0:DK])
                e.tensor_copy(DUDBG[64:106, :], ps_misc[64:106, 0:DK])
            tick("sV", e.tensor_scalar(DA[par][0:42, :], ps_misc[0:42, 0:DK],
                                       RC4[par][0:42, :], None, ALU.mult))
            tick("sV", e.tensor_scalar(DA[par][64:106, :],
                                       ps_misc[64:106, 0:DK],
                                       RC4[par][64:106, :], None, ALU.mult))
            for n in range(4):
                wait(e, "sT", T4_cx(hp, n))
                tick("sV", e.tensor_copy(CTX[:, hp, n * 512:(n + 1) * 512],
                                         ps_cx[n % 2][:, :]))

    blk.gpsimd(p4_gpsimd)
    blk.tensor(p4_tensor)
    blk.scalar(p4_scalar)
    blk.vector(p4_vector)
    p4.close()
    esR2.close()
    esA.close()

    if debug:
        dump([("ctx", CTX[:, :, :]), ("qst7", QST[1][:, :]), ("pt7", PT[1][:, :, :, :]),
              ("rc7", RC4[1][:, :]), ("da7", DA[1][:, :]), ("gaug7", GAUG[1][:, :]),
              ("gta7", GTA4[1][:, :, :, :]), ("e42", e42_row[:, :]),
              ("mvb", mv_row_b[:, :]), ("duraw", DUDBG[:, :]),
              ("borow4", bo_row[:, :])])
    if stop_after == "P4":
        return finish(esR1)

    A = sb_in(es, "A", [128, KC, L], F32)
    X1B = sb_in(es, "X1B", [128, KC, L], BF16)

    p5 = ExitStack()
    WO = sb_in(p5, "WO", [128, KC, D], BF16)
    WL5 = sb_in(p5, "WL5", [128, 2, D], F32)
    XT5 = [sb_in(p5, f"XT5_{i}", [128, D], F32) for i in range(2)]
    ST5 = sb_in(p5, "ST5", [128, 8], F32)
    LOGIT = sb_in(p5, "LOGIT", [128, LC, E], F32)
    GWRK = sb_in(p5, "GWRK", [128, 3, LC, E], F32)
    NGR = sb_in(p5, "NGR", [16, 128], F32)
    NGROW = sb_in(p5, "NGROW", [1, L], F32)
    RGB = sb_in(p5, "RGB", [128, L], F32)

    ps_o = [psum_in(p5, f"ps_o{i}", [128, 512]) for i in range(2)]
    ps_x1 = [psum_in(p5, f"ps_x1{i}", [128, 4, 128]) for i in range(2)]
    ps_gt = [psum_in(p5, f"ps_gt{i}", [128, E]) for i in range(2)]
    ps_ng = psum_in(p5, "ps_ng", [128, 128])
    ps_sq5 = psum_in(p5, "ps_sq5", [128, 512])

    P5S = dict(C)

    def T5_o(c, half):
        return P5S["sT"] + 5 * c + half + 1

    def T5_x1(c, kb):
        return P5S["sT"] + 5 * c + 3 + kb

    def T5_g(c):
        return P5S["sT"] + 5 * c + 5

    def V5_oev(c, half):
        return P5S["sV"] + 15 * c + half + 1

    def V5_r(c):
        return P5S["sV"] + 15 * c + 2

    def V5_var(c):
        return P5S["sV"] + 15 * c + 8

    def V5_app(c):
        return P5S["sV"] + 15 * c + 10

    def V5_x1ev(c, kb):
        return P5S["sV"] + 15 * c + 12 + 2 * kb

    def V5_g(c):
        return P5S["sV"] + 15 * c + 15

    def S5_cast(i):
        return P5S["sS"] + i + 1

    def S5_sq(c):
        return P5S["sS"] + 8 + 3 * c + 2

    def S5_sd(c):
        return P5S["sS"] + 8 + 3 * c + 3

    def D5_xt(c):
        return P5S["sD"] + 16 * (c + 1)

    def p5_gpsimd(e):
        barrier(e)
        for i in range(KC):
            if i >= 2:
                wait(e, "sS", S5_cast(i - 2))
            tick("gD", e.dma_start(
                WL5[:, i % 2, :],
                wo_d.rearrange("(k p) n -> p k n", p=128)[:, i, :]), 16)

    def p5_sync(e):
        barrier(e)
        for c in range(LC):
            if c >= 2:
                wait(e, "sT", T5_x1(c - 2, 1))
            tick("sD", e.dma_start(
                XT5[c % 2][:, :],
                x_d.rearrange("(c p) n -> p c n", p=128)[:, c, :]), 16)

    def p5_tensor(e):
        barrier(e, P5S)
        wait(e, "sS", P5S["sS"] + 8)
        for c in range(LC):
            par = c % 2
            for half in range(2):
                key = 2 * c + half
                if key >= 2:
                    wait(e, "sV", V5_oev((key - 2) // 2, (key - 2) % 2))
                for k in range(KC):
                    e.matmul(ps_o[key % 2][:, :],
                             CTX[:, k, c * 128:(c + 1) * 128],
                             WO[:, k, half * 512:(half + 1) * 512],
                             start=(k == 0), stop=False)
                tick("sT", e.matmul(ps_o[key % 2][:, :], ones_row_b[:, :],
                                    bo_row[0:1, half * 512:(half + 1) * 512],
                                    start=False, stop=True))
            wait(e, "sV", V5_app(c))
            for kb in range(2):
                key = 2 * c + kb
                if key >= 2:
                    wait(e, "sV", V5_x1ev((key - 2) // 2, (key - 2) % 2))
                for k in range(4):
                    kk = kb * 4 + k
                    instr = e.transpose(ps_x1[key % 2][:, k, :],
                                        XT5[par][:, kk * 128:(kk + 1) * 128],
                                        ident_f[:, :])
                    if k == 3:
                        tick("sT", instr)
            wait(e, "sV", V5_x1ev(c, 1))
            if c >= 2:
                wait(e, "sV", V5_g(c - 2))
            for k in range(KC):
                instr = e.matmul(ps_gt[par][:, :],
                                 X1B[:, k, c * 128:(c + 1) * 128],
                                 gate_wT[:, k, :],
                                 start=(k == 0), stop=(k == KC - 1))
                if k == KC - 1:
                    tick("sT", instr)

    def p5_scalar(e):
        for i in range(KC):
            wait(e, "gD", P5S["gD"] + 16 * (i + 1))
            tick("sS", e.copy(WO[:, i, :], WL5[:, i % 2, :]))
        for c in range(LC):
            par = c % 2
            wait(e, "sV", V5_r(c))
            tick("sS", e.activation(ps_sq5[:, :], XT5[par][:, 0:512], AF.Square,
                                    bias=0.0, scale=1.0,
                                    accum_out=ST5[:, 6:7]))
            tick("sS", e.activation(ps_sq5[:, :], XT5[par][:, 512:1024], AF.Square,
                                    bias=0.0, scale=1.0,
                                    accum_out=ST5[:, 7:8]))
            wait(e, "sV", V5_var(c))
            tick("sS", e.activation(ST5[:, 4:5], ST5[:, 3:4], AF.Sqrt,
                                    bias=eps_col[:, :], scale=1.0))

    def p5_vector(e):
        for c in range(LC):
            par = c % 2
            for half in range(2):
                key = 2 * c + half
                wait(e, "sT", T5_o(c, half))
                wait(e, "sD", D5_xt(c))
                if c >= 2:
                    wait(e, "sS", S5_sq(c - 2))
                tick("sV", e.tensor_add(XT5[par][:, half * 512:(half + 1) * 512],
                                        ps_o[key % 2][:, :],
                                        XT5[par][:, half * 512:(half + 1) * 512]))
            tick("sV", e.reduce_sum(ST5[:, 0:1], XT5[par][:, :], axis=AX.X))
            wait(e, "sS", S5_sq(c))
            tick("sV", e.tensor_add(ST5[:, 1:2], ST5[:, 6:7], ST5[:, 7:8]))
            vsp(e)
            tick("sV", e.tensor_single_scalar(ST5[:, 0:1], ST5[:, 0:1], 1.0 / D,
                                              ALU.mult))
            vsp(e)
            tick("sV", e.tensor_single_scalar(ST5[:, 1:2], ST5[:, 1:2], 1.0 / D,
                                              ALU.mult))
            vsp(e)
            tick("sV", e.tensor_mul(ST5[:, 2:3], ST5[:, 0:1], ST5[:, 0:1]))
            vsp(e)
            tick("sV", e.tensor_sub(ST5[:, 3:4], ST5[:, 1:2], ST5[:, 2:3]))
            wait(e, "sS", S5_sd(c))
            tick("sV", e.reciprocal(ST5[:, 5:6], ST5[:, 4:5]))
            vsp(e)
            tick("sV", e.tensor_scalar(XT5[par][:, :], XT5[par][:, :],
                                       ST5[:, 0:1], ST5[:, 5:6],
                                       ALU.subtract, ALU.mult))
            for kb in range(2):
                key = 2 * c + kb
                wait(e, "sT", T5_x1(c, kb))
                tick("sV", e.tensor_copy(
                    A[:, kb * 4:(kb + 1) * 4, c * 128:(c + 1) * 128],
                    ps_x1[key % 2][:, :, :]))
                tick("sV", e.tensor_copy(
                    X1B[:, kb * 4:(kb + 1) * 4, c * 128:(c + 1) * 128],
                    ps_x1[key % 2][:, :, :]))
            wait(e, "sT", T5_g(c))
            tick("sV", e.tensor_add(LOGIT[:, c, :], ps_gt[par][:, :],
                                    gate_b_bc[:, :]))

    blk.gpsimd(p5_gpsimd)
    blk.sync(p5_sync)
    blk.tensor(p5_tensor)
    blk.scalar(p5_scalar)
    blk.vector(p5_vector)

    def v_gate1(e):
        wait(e, "sV")
        wait(e, "sS")
        tick("sV", e.tensor_mul(GWRK[:, 0, :, :], LOGIT[:, :, :], LOGIT[:, :, :]))
        vsp(e)
        tick("sV", e.reduce_sum(NG[:, :], GWRK[:, 0, :, :], axis=AX.X))
        vsp(e)
        tick("sV", e.tensor_scalar(NG[:, :], NG[:, :], 1.0 / E, EPS_RMS,
                                   ALU.mult, ALU.add))
    blk.vector(v_gate1)

    def s_gate1(e):
        wait(e, "sV")
        tick("sS", e.activation(NG[:, :], NG[:, :], AF.Sqrt, bias=0.0, scale=1.0))
    blk.scalar(s_gate1)

    def v_gate2(e):
        wait(e, "sS")
        tick("sV", e.reciprocal(NG[:, :], NG[:, :]))
        vsp(e)
        tick("sV", e.tensor_mul(GWRK[:, 0, :, :], LOGIT[:, :, :],
                                NG[:, :, None].broadcast_to([128, LC, E])))
        vsp(e)
        tick("sV", e.tensor_mul(GWRK[:, 0, :, :], GWRK[:, 0, :, :],
                                rms_w_bc[:, None, :].broadcast_to([128, LC, E])))
        vsp(e)
        tick("sV", e.reduce_max(NG[:, :], GWRK[:, 0, :, :], axis=AX.X))
        vsp(e)
        tick("sV", e.tensor_sub(GWRK[:, 1, :, :], GWRK[:, 0, :, :],
                                NG[:, :, None].broadcast_to([128, LC, E])))
    blk.vector(v_gate2)

    def s_gate2(e):
        wait(e, "sV")
        tick("sS", e.activation(GWRK[:, 2, :, :], GWRK[:, 1, :, :], AF.Exp,
                                bias=0.0, scale=2.0))
    blk.scalar(s_gate2)

    def v_gate3(e):
        wait(e, "sS")
        tick("sV", e.reduce_sum(NG[:, :], GWRK[:, 2, :, :], axis=AX.X))
        vsp(e)
        tick("sV", e.tensor_scalar(NG[:, :], NG[:, :], 1e-6, 1.0,
                                   ALU.mult, ALU.add))
        vsp(e)
        tick("sV", e.reciprocal(NG[:, :], NG[:, :]))
    blk.vector(v_gate3)

    def t_gate(e):
        wait(e, "sV")
        tick("sT", e.transpose(ps_ng[0:16, :], NG[:, :], ident_f[:, :]))
    blk.tensor(t_gate)

    def v_gate4(e):
        wait(e, "sT")
        tick("sV", e.tensor_copy(NGR[:, :], ps_ng[0:16, :]))
    blk.vector(v_gate4)

    def g_gate(e):
        wait(e, "sV")
        tick("gD", e.dma_start(NGROW[0:1, :], NGR[:, :]), 16)
    blk.gpsimd(g_gate)

    def v_gate5(e):
        wait(e, "gD")
        tick("sV", e.reciprocal(NGROW[0:1, :], NGROW[0:1, :]))
    blk.vector(v_gate5)

    for _n in range(4):
        def t_bc(e, n=_n):
            wait(e, "sV")
            tick("sT", e.matmul(ps_o[n % 2][:, :], ones_row_f[:, :],
                                NGROW[0:1, n * 512:(n + 1) * 512],
                                start=True, stop=True))
        blk.tensor(t_bc)

        def v_bc(e, n=_n):
            wait(e, "sT")
            tick("sV", e.tensor_copy(RGB[:, n * 512:(n + 1) * 512],
                                     ps_o[n % 2][:, :]))
        blk.vector(v_bc)

    def v_ascale(e):
        vsp(e)
        for k in range(KC):
            tick("sV", e.tensor_mul(A[:, k, :], A[:, k, :], RGB[:, :]))
    blk.vector(v_ascale)

    if debug:
        dump([("x1b", X1B[:, :, :]), ("ng", NG[:, :]),
              ("logit", LOGIT[:, :, :]), ("borow5", bo_row[:, :])])
    if stop_after == "P5":
        return finish(p5, esR1)

    p5.close()
    esR1.close()

    p6 = ExitStack()
    WF = [sb_in(p6, f"WF{i}", [128, 16, D], BF16) for i in range(2)]
    WL6 = sb_in(p6, "WL6", [128, 2, D], F32)
    HSB = sb_in(p6, "HSB", [128, 2, KC, TF], BF16, side="right")
    ps_h = [psum_in(p6, f"ps_h{i}", [128, TF]) for i in range(2)]
    ps_y = psum_in(p6, "ps_y", [128, 4, TF])

    P6S = dict(C)
    TPH = NTF * 16

    def G6c(he, i):
        return P6S["gD"] + 16 * (16 * he + i + 1)

    def T6_mm1(he, t, m):
        return P6S["sT"] + he * TPH + t * 16 + m + 1

    def T6_mm2(he, t, ps, j):
        return P6S["sT"] + he * TPH + t * 16 + 8 + ps * 4 + j + 1

    def S6_cast(he, j):
        return P6S["sS"] + 40 * he + j + 1

    def S6(he, t, m):
        return P6S["sS"] + 40 * he + 8 + t * 8 + m + 1

    def V6_cast(he, j):
        return P6S["sV"] + 40 * he + j + 1

    def V6(he, t, ps, j):
        return P6S["sV"] + 40 * he + 8 + (t * 2 + ps) * 4 + j + 1

    def p6_gpsimd(e):
        barrier(e)
        for he in range(NHE):
            ex, half = he // 2, he % 2
            if he >= 2:
                wait(e, "sT", P6S["sT"] + (he - 1) * TPH)
            w1v = w1_d[ex, :, half * 1024:(half + 1) * 1024] \
                .rearrange("(k p) n -> p k n", p=128)
            w2v = w2_d[ex, half * 1024:(half + 1) * 1024, :] \
                .rearrange("(k p) n -> p k n", p=128)
            for i in range(16):
                if 16 * he + i >= 2:
                    gi = 16 * he + i - 2
                    he2, i2 = divmod(gi, 16)
                    if i2 < 8:
                        wait(e, "sS", S6_cast(he2, i2))
                    else:
                        wait(e, "sV", V6_cast(he2, i2 - 8))
                srcv = w1v[:, i, :] if i < 8 else w2v[:, i - 8, :]
                tick("gD", e.dma_start(WL6[:, i % 2, :], srcv), 16)

    def p6_tensor(e):
        for name in ["sV", "sS", "sD"]:
            wait(e, name, P6S[name])
        for he in range(NHE):
            wait(e, "sS", P6S["sS"] + 40 * he + 8)
            wait(e, "sV", P6S["sV"] + 40 * he + 8)
            for t in range(NTF):
                for m in range(KC):
                    q = (he * NTF + t) * 8 + m
                    if q >= 2:
                        q2 = q - 2
                        he2, r2 = divmod(q2, NTF * 8)
                        t2, m2 = divmod(r2, 8)
                        wait(e, "sS", S6(he2, t2, m2))
                    for k in range(KC):
                        instr = e.matmul(ps_h[q % 2][:, :],
                                         WF[he % 2][:, k, m * 128:(m + 1) * 128],
                                         X1B[:, k, t * TF:(t + 1) * TF],
                                         start=(k == 0), stop=(k == KC - 1))
                        if k == KC - 1:
                            tick("sT", instr)
                wait(e, "sS", S6(he, t, 7))
                for ps in range(2):
                    for j in range(4):
                        r = (he * NTF + t) * 2 + ps
                        if r >= 1:
                            r2 = r - 1
                            het2, ps2 = divmod(r2, 2)
                            he2, t2 = divmod(het2, NTF)
                            wait(e, "sV", V6(he2, t2, ps2, j))
                        for hk in range(KC):
                            instr = e.matmul(
                                ps_y[:, j, :],
                                WF[he % 2][:, 8 + hk,
                                           (ps * 4 + j) * 128:(ps * 4 + j + 1) * 128],
                                HSB[:, t % 2, hk, :],
                                start=(hk == 0), stop=(hk == KC - 1))
                            if hk == KC - 1:
                                tick("sT", instr)

    def p6_scalar(e):
        for he in range(NHE):
            ex, half = he // 2, he % 2
            for j in range(KC):
                wait(e, "gD", G6c(he, j))
                tick("sS", e.copy(WF[he % 2][:, j, :], WL6[:, j % 2, :]))
            for t in range(NTF):
                for m in range(KC):
                    wait(e, "sT", T6_mm1(he, t, m))
                    q = (he * NTF + t) * 8 + m
                    tick("sS", e.activation(HSB[:, t % 2, m, :],
                                            ps_h[q % 2][:, :], AF.Gelu,
                                            bias=b1c[:, half * 8 + m, ex:ex + 1],
                                            scale=1.0))

    def p6_vector(e):
        for he in range(NHE):
            for j in range(KC):
                wait(e, "gD", G6c(he, 8 + j))
                tick("sV", e.tensor_copy(WF[he % 2][:, 8 + j, :],
                                         WL6[:, j % 2, :]))
            for t in range(NTF):
                for ps in range(2):
                    for j in range(4):
                        wait(e, "sT", T6_mm2(he, t, ps, j))
                        d = ps * 4 + j
                        tick("sV", e.tensor_add(A[:, d, t * TF:(t + 1) * TF],
                                                ps_y[:, j, :],
                                                A[:, d, t * TF:(t + 1) * TF]))

    blk.gpsimd(p6_gpsimd)
    blk.tensor(p6_tensor)
    blk.scalar(p6_scalar)
    blk.vector(p6_vector)
    p6.close()

    if debug:
        dump([("afin", A[:, :, :])])
    if stop_after == "P6":
        return finish()

    p7 = ExitStack()
    OUT5 = [sb_in(p7, f"OUT5_{i}", [128, D], F32) for i in range(2)]
    ST7 = sb_in(p7, "ST7", [128, 10], F32)
    GG = sb_in(p7, "GG", [128, LC], F32)
    ps_7 = [psum_in(p7, f"ps_7{i}", [128, D]) for i in range(2)]
    ps_sq7 = psum_in(p7, "ps_sq7", [128, 512])

    P7S = dict(C)

    def T7(c):
        return P7S["sT"] + c + 1

    def V7pre_end():
        return P7S["sV"] + KC + 1

    def V7_var(c):
        return P7S["sV"] + KC + 1 + 12 * c + 8

    def V7(c):
        return P7S["sV"] + KC + 1 + 12 * (c + 1)

    def S7_sq(c):
        return P7S["sS"] + 3 * c + 2

    def S7_sd(c):
        return P7S["sS"] + 3 * c + 3

    def D7(c):
        return P7S["sD"] + 16 * (c + 1)

    def p7_vector_pre(e):
        for k in range(KC):
            tick("sV", e.tensor_scalar(A[:, k, :], A[:, k, :],
                                       sb2c[:, k:k + 1], None, ALU.add))
        tick("sV", e.tensor_mul(GG[:, :], NG[:, :], NG[:, :]))
    blk.vector(p7_vector_pre)

    def p7_tensor(e):
        wait(e, "sV", V7pre_end())
        for c in range(LC):
            if c >= 2:
                wait(e, "sV", V7(c - 2))
            for k in range(KC):
                instr = e.transpose(ps_7[c % 2][:, k * 128:(k + 1) * 128],
                                    A[:, k, c * 128:(c + 1) * 128],
                                    ident_f[:, :])
                if k == KC - 1:
                    tick("sT", instr)
    blk.tensor(p7_tensor)

    def p7_scalar(e):
        for c in range(LC):
            wait(e, "sT", T7(c))
            if c >= 1:
                wait(e, "sV", V7_var(c - 1))
            tick("sS", e.activation(ps_sq7[:, :], ps_7[c % 2][:, 0:512],
                                    AF.Square, bias=0.0, scale=1.0,
                                    accum_out=ST7[:, 6:7]))
            tick("sS", e.activation(ps_sq7[:, :], ps_7[c % 2][:, 512:1024],
                                    AF.Square, bias=0.0, scale=1.0,
                                    accum_out=ST7[:, 7:8]))
            wait(e, "sV", V7_var(c))
            tick("sS", e.activation(ST7[:, 4:5], ST7[:, 3:4], AF.Sqrt,
                                    bias=eps_col[:, :], scale=1.0))
    blk.scalar(p7_scalar)

    def p7_vector(e):
        for c in range(LC):
            wait(e, "sT", T7(c))
            tick("sV", e.reduce_sum(ST7[:, 0:1], ps_7[c % 2][:, :], axis=AX.X))
            wait(e, "sS", S7_sq(c))
            tick("sV", e.tensor_add(ST7[:, 1:2], ST7[:, 6:7], ST7[:, 7:8]))
            vsp(e)
            tick("sV", e.tensor_mul(ST7[:, 0:1], ST7[:, 0:1], NG[:, c:c + 1]))
            vsp(e)
            tick("sV", e.tensor_mul(ST7[:, 1:2], ST7[:, 1:2], GG[:, c:c + 1]))
            vsp(e)
            tick("sV", e.tensor_single_scalar(ST7[:, 0:1], ST7[:, 0:1], 1.0 / D,
                                              ALU.mult))
            vsp(e)
            tick("sV", e.tensor_single_scalar(ST7[:, 1:2], ST7[:, 1:2], 1.0 / D,
                                              ALU.mult))
            vsp(e)
            tick("sV", e.tensor_mul(ST7[:, 2:3], ST7[:, 0:1], ST7[:, 0:1]))
            vsp(e)
            tick("sV", e.tensor_sub(ST7[:, 3:4], ST7[:, 1:2], ST7[:, 2:3]))
            wait(e, "sS", S7_sd(c))
            tick("sV", e.reciprocal(ST7[:, 5:6], ST7[:, 4:5]))
            vsp(e)
            tick("sV", e.tensor_mul(ST7[:, 8:9], NG[:, c:c + 1], ST7[:, 5:6]))
            vsp(e)
            tick("sV", e.tensor_mul(ST7[:, 9:10], ST7[:, 0:1], ST7[:, 5:6]))
            vsp(e)
            if c >= 2:
                wait(e, "sD", D7(c - 2))
            tick("sV", e.tensor_scalar(OUT5[c % 2][:, :], ps_7[c % 2][:, :],
                                       ST7[:, 8:9], ST7[:, 9:10],
                                       ALU.mult, ALU.subtract))
    blk.vector(p7_vector)

    def p7_sync(e):
        for c in range(LC):
            wait(e, "sV", V7(c))
            tick("sD", e.dma_start(
                out_d.rearrange("(c p) n -> p c n", p=128)[:, c, :],
                OUT5[c % 2][:, :]), 16)
        wait(e, "sD")
    blk.sync(p7_sync)
    p7.close()

    return finish()


def _shard_inputs(inputs):
    x = np.ascontiguousarray(np.asarray(inputs["L0_attn_x"], np.float32))
    names = {
        "W_q": "L0_attn_W_q", "b_q": "L0_attn_b_q",
        "W_k": "L0_attn_W_k", "b_k": "L0_attn_b_k",
        "W_v": "L0_attn_W_v", "b_v": "L0_attn_b_v",
        "W_o": "L0_attn_W_o", "b_o": "L0_attn_b_o",
        "gate_w": "L0_ffn_gate_w", "gate_b": "L0_ffn_gate_b",
        "rms_w": "L0_ffn_rms_w", "w1": "L0_ffn_w1", "b1": "L0_ffn_b1",
        "w2": "L0_ffn_w2", "b2": "L0_ffn_b2",
    }
    base = {k: np.ascontiguousarray(np.asarray(inputs[v], np.float32))
            for k, v in names.items()}
    base["ksi"] = np.ascontiguousarray(
        np.asarray(inputs["L0_attn_k_sample_idx"], np.int64).astype(np.float32))
    in_maps = []
    for c in range(8):
        m = dict(base)
        m["x"] = np.ascontiguousarray(x[c])
        in_maps.append(m)
    return in_maps


def kernel(**inputs):
    in_maps = _shard_inputs(inputs)
    nc = build_nc()
    res = run_bass_kernel_spmd(nc, in_maps, core_ids=list(range(8)))
    out = np.stack([np.asarray(r["out"], np.float32) for r in res.results],
                   axis=0)
    return out


# version 08 (final)
# speedup vs baseline: 1.0953x; 1.0953x over previous
import numpy as np
import ml_dtypes
import concourse.bass as bass
import concourse.mybir as mybir
from concourse.bass_utils import run_bass_kernel_spmd
from contextlib import ExitStack

F32 = mybir.dt.float32
BF16 = mybir.dt.bfloat16
I32 = mybir.dt.int32
AF = mybir.ActivationFunctionType
ALU = mybir.AluOpType
AX = mybir.AxisListType

L = 2048
D = 1024
H = 16
DK = 64
U = 41
E = 8
HD = 2048
LC = 16
KC = 8
TF = 512
NTF = L // TF
NHE = 2 * E
EPS_LN = 1e-5
EPS_RMS = float(np.finfo(np.float32).eps)


def build_nc(stop_after="P7", debug=()):
    nc = bass.Bass("TRN2", target_bir_lowering=False)
    es = ExitStack()

    x_d = nc.dram_tensor("x", [L, D], F32, kind="ExternalInput").ap()
    wq_d = nc.dram_tensor("W_q", [D, D], F32, kind="ExternalInput").ap()
    bq_d = nc.dram_tensor("b_q", [D], F32, kind="ExternalInput").ap()
    wk_d = nc.dram_tensor("W_k", [D, D], F32, kind="ExternalInput").ap()
    bk_d = nc.dram_tensor("b_k", [D], F32, kind="ExternalInput").ap()
    wv_d = nc.dram_tensor("W_v", [D, D], F32, kind="ExternalInput").ap()
    bv_d = nc.dram_tensor("b_v", [D], F32, kind="ExternalInput").ap()
    wo_d = nc.dram_tensor("W_o", [D, D], F32, kind="ExternalInput").ap()
    bo_d = nc.dram_tensor("b_o", [D], F32, kind="ExternalInput").ap()
    ksi_d = nc.dram_tensor("ksi", [U], F32, kind="ExternalInput").ap()
    gw_d = nc.dram_tensor("gate_w", [E, D], F32, kind="ExternalInput").ap()
    gb_d = nc.dram_tensor("gate_b", [E], F32, kind="ExternalInput").ap()
    rw_d = nc.dram_tensor("rms_w", [E], F32, kind="ExternalInput").ap()
    w1_d = nc.dram_tensor("w1", [E, D, HD], F32, kind="ExternalInput").ap()
    b1_d = nc.dram_tensor("b1", [E, HD], F32, kind="ExternalInput").ap()
    w2_d = nc.dram_tensor("w2", [E, HD, D], F32, kind="ExternalInput").ap()
    b2_d = nc.dram_tensor("b2", [E, D], F32, kind="ExternalInput").ap()
    out_d = nc.dram_tensor("out", [L, D], F32, kind="ExternalOutput").ap()

    dbg = {}

    def mkdbg(name, shape, dtype=F32):
        if name in debug:
            dbg[name] = nc.dram_tensor("dbg_" + name, shape, dtype,
                                       kind="ExternalOutput").ap()

    mkdbg("xtm", [128, LC, D], BF16)
    mkdbg("gst", [128, LC, U], BF16)
    mkdbg("idxbc", [128, U])
    mkdbg("idxf", [1, U])
    mkdbg("lcols", [128, LC])
    mkdbg("xfm", [128, KC, L], BF16)
    mkdbg("xs", [128, KC, U], BF16)
    mkdbg("qtm", [128, LC, D], BF16)
    mkdbg("kfm", [128, KC, L], BF16)
    mkdbg("vtm", [128, LC, D], BF16)
    mkdbg("ks", [128, E, U], BF16)
    mkdbg("meanv", [1, D])
    mkdbg("mpack", [128, H, LC])
    mkdbg("qf2", [128, LC, 128], BF16)
    mkdbg("mhm", [16, L])
    mkdbg("ks2", [128, E, U], BF16)
    mkdbg("tau", [16, 48])
    mkdbg("rstar", [128, H, LC])
    mkdbg("ctx", [128, KC, L], BF16)
    mkdbg("qst7", [128, U], BF16)
    mkdbg("pt7", [128, 2, LC, U], BF16)
    mkdbg("rc7", [128, 1])
    mkdbg("da7", [128, DK], BF16)
    mkdbg("gaug7", [128, L], BF16)
    mkdbg("gta7", [128, 2, LC, 42], BF16)
    mkdbg("e42", [1, 42], BF16)
    mkdbg("mvb", [1, D], BF16)
    mkdbg("duraw", [128, DK])
    mkdbg("x1b", [128, KC, L], BF16)
    mkdbg("ng", [128, LC])
    mkdbg("logit", [128, LC, E])
    mkdbg("afin", [128, KC, L])
    for _bn in range(1, 6):
        mkdbg(f"borow{_bn}", [1, D], BF16)

    sems, C = {}, {}
    for name in ["sD", "gD", "sT", "sV", "sS", "sG", "sE"]:
        sems[name] = es.enter_context(nc.semaphore(name))
        C[name] = 0

    def tick(name, instr, n=1):
        instr.then_inc(sems[name], n)
        C[name] += n
        return C[name]

    def wait(eng, name, val=None):
        v = C[name] if val is None else val
        if v > 0:
            eng.wait_ge(sems[name], v)

    def vsp(e):
        wait(e, "sV")

    def barrier(eng, snap=None):
        src = C if snap is None else snap
        for name in sems:
            if src[name] > 0:
                eng.wait_ge(sems[name], src[name])

    def sb_in(stack, name, shape, dtype, side=None):
        return stack.enter_context(nc.sbuf_tensor(name, shape, dtype, side=side))

    def psum_in(stack, name, shape, dtype=F32):
        return stack.enter_context(nc.psum_tensor(name, shape, dtype))

    ident_f = sb_in(es, "ident_f", [128, 128], F32)
    ident_b = sb_in(es, "ident_b", [128, 128], BF16)
    tri_b = sb_in(es, "tri_b", [128, 128], BF16)
    iota_c = sb_in(es, "iota_c", [128, 1], F32)
    lcols = sb_in(es, "lcols", [128, LC], F32)
    ones_row_b = sb_in(es, "ones_row_b", [1, 128], BF16)
    ones_row_f = sb_in(es, "ones_row_f", [1, 128], F32)
    ones_col_b = sb_in(es, "ones_col_b", [128, 1], BF16)
    u_iota_bc = sb_in(es, "u_iota_bc", [128, U], F32)
    u42_bc = sb_in(es, "u42_bc", [128, 42], F32)
    ic42 = sb_in(es, "ic42", [128, 1], F32)
    e42_row = sb_in(es, "e42_row", [1, 42], BF16)
    zeros16 = sb_in(es, "zeros16", [128, LC], F32)
    zrow42 = sb_in(es, "zrow42", [1, 42], BF16)
    bo_row = sb_in(es, "bo_row", [1, D], BF16)
    bk_cols = sb_in(es, "bk_cols", [128, KC], F32)
    gate_wT = sb_in(es, "gate_wT", [128, KC, E], BF16)
    gate_b_bc = sb_in(es, "gate_b_bc", [128, E], F32)
    rms_w_bc = sb_in(es, "rms_w_bc", [128, E], F32)
    b1c = sb_in(es, "b1c", [128, 16, E], F32)
    sb2c = sb_in(es, "sb2c", [128, KC], F32)
    idx_f = sb_in(es, "idx_f", [1, U], F32)
    idx_bc = sb_in(es, "idx_bc", [128, U], F32)
    i42m = sb_in(es, "i42m", [128, 1], F32)
    NG = sb_in(es, "NG", [128, LC], F32)
    eps_col = sb_in(es, "eps_col", [128, 1], F32)
    SPC = sb_in(es, "SPC", [128, 512], F32)

    esA = ExitStack()
    QTM = sb_in(esA, "QTM", [128, LC, D], BF16)
    KFM = sb_in(esA, "KFM", [128, KC, L], BF16)
    VTM = sb_in(esA, "VTM", [128, LC, D], BF16)
    mv_row_f = sb_in(esA, "mv_row_f", [1, D], F32)
    mv_row_b = sb_in(esA, "mv_row_b", [1, D], BF16)
    esB = ExitStack()
    KS = sb_in(esB, "KS", [128, E, U], BF16)
    esC = ExitStack()
    XS = sb_in(esC, "XS", [128, KC, U], BF16)
    GsT = sb_in(esC, "GsT", [128, LC, U], BF16)
    XFM = sb_in(esC, "XFM", [128, KC, L], BF16)
    bq_row = sb_in(esC, "bq_row", [1, D], BF16)
    bv_row = sb_in(esC, "bv_row", [1, D], BF16)

    esW = ExitStack()

    blk = es.enter_context(nc.Block())

    p0 = ExitStack()
    irow128 = sb_in(p0, "irow128", [128, 128], F32)
    stage2_f = sb_in(p0, "stage2_f", [E, D], F32)
    b1s = sb_in(p0, "b1s", [E, HD], F32)
    e42f = sb_in(p0, "e42f", [1, 42], F32)
    b2s = sb_in(p0, "b2s", [E, D], F32)
    bq_st = sb_in(p0, "bq_st", [1, D], F32)
    bk_st = sb_in(p0, "bk_st", [1, D], F32)
    bv_st = sb_in(p0, "bv_st", [1, D], F32)
    bo_st = sb_in(p0, "bo_st", [1, D], F32)
    gb_st = sb_in(p0, "gb_st", [1, E], F32)
    rw_st = sb_in(p0, "rw_st", [1, E], F32)
    ps_bk = psum_in(p0, "ps_bk", [128, KC])
    ps_gw = psum_in(p0, "ps_gw", [128, KC, E])
    ps_b1 = psum_in(p0, "ps_b1", [128, 16, E])
    ps_b2 = psum_in(p0, "ps_b2", [128, KC, E])
    ps_bc = psum_in(p0, "ps_bc", [128, U + 2 * E])

    def p0_sync(e):
        tick("sD", e.dma_start(idx_f[:, :], ksi_d.rearrange("(a u) -> a u", a=1)), 16)
        tick("sD", e.dma_start(bq_st[:, :], bq_d.rearrange("(a u) -> a u", a=1)), 16)
        tick("sD", e.dma_start(bk_st[:, :], bk_d.rearrange("(a u) -> a u", a=1)), 16)
        tick("sD", e.dma_start(bv_st[:, :], bv_d.rearrange("(a u) -> a u", a=1)), 16)
        tick("sD", e.dma_start(bo_st[:, :], bo_d.rearrange("(a u) -> a u", a=1)), 16)
        tick("sD", e.dma_start(gb_st[:, :], gb_d.rearrange("(a u) -> a u", a=1)), 16)
        tick("sD", e.dma_start(rw_st[:, :], rw_d.rearrange("(a u) -> a u", a=1)), 16)
        tick("sD", e.dma_start(b2s[:, :], b2_d), 16)
        tick("sD", e.dma_start(stage2_f[0:E, 0:D], gw_d), 16)
        tick("sD", e.dma_start(b1s[:, :], b1_d), 16)

    def p0_gpsimd(e):
        kw = dict(allow_small_or_imprecise_dtypes=True)
        tick("sG", e.iota(iota_c[:, :], [[0, 1]], base=0, channel_multiplier=1, **kw))
        tick("sG", e.iota(lcols[:, :], [[128, LC]], base=0, channel_multiplier=1, **kw))
        tick("sG", e.iota(irow128[:, :], [[1, 128]], base=0, channel_multiplier=0, **kw))
        tick("sG", e.iota(u_iota_bc[:, :], [[1, U]], base=1, channel_multiplier=0, **kw))
        tick("sG", e.iota(u42_bc[:, :], [[1, 42]], base=1, channel_multiplier=0, **kw))
        wait(e, "sG")
        tick("sG", e.memset(u42_bc[:, 41:42], 0.0))
        tick("sG", e.iota(ic42[:, :], [[0, 1]], base=1, channel_multiplier=1, **kw))
        tick("sG", e.memset(zeros16[:, :], 0.0))
        tick("sG", e.memset(zrow42[:, :], 0.0))
        tick("sG", e.memset(ones_row_b[:, :], 1.0))
        tick("sG", e.memset(ones_row_f[:, :], 1.0))
        tick("sG", e.memset(ones_col_b[:, :], 1.0))

        tick("sG", e.memset(eps_col[:, :], EPS_LN))
        tick("sG", e.memset(SPC[:, :], 0.0))

    def p0_vector(e):
        wait(e, "sG")
        tick("sV", e.tensor_scalar(ident_f[:, :], irow128[:, :], iota_c[:, :],
                                   None, ALU.is_equal))
        vsp(e)
        tick("sV", e.tensor_copy(ident_b[:, :], ident_f[:, :]))
        tick("sV", e.tensor_scalar(tri_b[:, :], irow128[:, :], iota_c[:, :],
                                   None, ALU.is_gt))
        tick("sV", e.tensor_single_scalar(ic42[64:128, :], ic42[64:128, :], 64.0,
                                          ALU.subtract))
        vsp(e)
        tick("sV", e.tensor_single_scalar(i42m[:, :], ic42[:, :], 42.0,
                                          ALU.is_equal))
        vsp(e)
        tick("sV", e.scalar_tensor_tensor(ic42[:, :], i42m[:, :], -42.0,
                                          ic42[:, :], ALU.mult, ALU.add))
        tick("sV", e.tensor_single_scalar(e42f[:, :], irow128[0:1, 0:42], 41.0,
                                          ALU.is_equal))
        vsp(e)
        tick("sV", e.tensor_copy(e42_row[:, :], e42f[:, :]))
        wait(e, "sD")
        tick("sV", e.tensor_single_scalar(idx_f[:, :], idx_f[:, :], float(L - 1),
                                          ALU.min))
        tick("sV", e.engine_nop())
        tick("sV", e.tensor_copy(bq_row[:, :], bq_st[:, :]))
        tick("sV", e.tensor_copy(bv_row[:, :], bv_st[:, :]))
        tick("sV", e.tensor_copy(bo_row[:, :], bo_st[:, :]))

    def p0_tensor(e):
        wait(e, "sV")
        for m in range(KC):
            tick("sT", e.transpose(ps_bk[:, m:m + 1],
                                   bk_st[0:1, m * 128:(m + 1) * 128],
                                   ident_f[0:1, 0:1]))
        for k in range(KC):
            tick("sT", e.transpose(ps_gw[:, k, :], stage2_f[:, k * 128:(k + 1) * 128],
                                   ident_f[0:E, 0:E]))
        for hc in range(16):
            tick("sT", e.transpose(ps_b1[:, hc, :], b1s[:, hc * 128:(hc + 1) * 128],
                                   ident_f[0:E, 0:E]))
        for m in range(KC):
            tick("sT", e.transpose(ps_b2[:, m, :],
                                   b2s[:, m * 128:(m + 1) * 128],
                                   ident_f[0:E, 0:E]))
        tick("sT", e.matmul(ps_bc[:, 0:U], ones_row_f[:, :], idx_f[0:1, :],
                            start=True, stop=True))
        tick("sT", e.matmul(ps_bc[:, U:U + E], ones_row_f[:, :], gb_st[0:1, :],
                            start=True, stop=True))
        tick("sT", e.matmul(ps_bc[:, U + E:U + 2 * E], ones_row_f[:, :],
                            rw_st[0:1, :], start=True, stop=True))

    def p0_vector2(e):
        wait(e, "sT")
        tick("sV", e.tensor_copy(bk_cols[:, :], ps_bk[:, :]))
        for k in range(KC):
            tick("sV", e.tensor_copy(gate_wT[:, k, :], ps_gw[:, k, :]))
        for hc in range(16):
            tick("sV", e.tensor_copy(b1c[:, hc, :], ps_b1[:, hc, :]))
        for m in range(KC):
            tick("sV", e.reduce_sum(sb2c[:, m:m + 1], ps_b2[:, m, :], axis=AX.X))
        tick("sV", e.tensor_copy(idx_bc[:, :], ps_bc[:, 0:U]))
        tick("sV", e.tensor_copy(gate_b_bc[:, :], ps_bc[:, U:U + E]))
        tick("sV", e.tensor_copy(rms_w_bc[:, :], ps_bc[:, U + E:U + 2 * E]))
        vsp(e)
        for c in range(LC):
            tick("sV", e.tensor_scalar(GsT[:, c, :], idx_bc[:, :],
                                       lcols[:, c:c + 1], None, ALU.is_equal))

    blk.sync(p0_sync)
    blk.gpsimd(p0_gpsimd)
    blk.vector(p0_vector)
    blk.tensor(p0_tensor)
    blk.vector(p0_vector2)
    p0.close()

    p1 = ExitStack()
    XTM = sb_in(p1, "XTM", [128, LC, D], BF16)
    XT1 = [sb_in(p1, f"XT1_{i}", [128, D], F32) for i in range(2)]
    ps_tr = [psum_in(p1, f"ps_tr{i}", [128, 4, 128], BF16) for i in range(2)]
    ps_xs = [psum_in(p1, f"ps_xs{i}", [128, U]) for i in range(2)]

    sT0p1 = C["sT"]
    sV0p1 = C["sV"]
    sD0p1 = C["sD"]

    def _dma_even(k, n):
        k = min(k + (k & 1), n)
        return 16 * k

    def D_x1(c):
        return sD0p1 + _dma_even(c + 1, LC)

    def V_cast1(c):
        return sV0p1 + c + 1

    def T_tr1(ci):
        return sT0p1 + 4 * (ci + 1)

    def T_xs1(k):
        return sT0p1 + 128 + 16 * (k + 1)

    def V_tr1(ci):
        return sV0p1 + 16 + ci + 1

    def V_xs1(k):
        return sV0p1 + 48 + k + 1

    def p1_sync(e):
        barrier(e)
        for c in range(LC):
            if c >= 2:
                wait(e, "sV", V_cast1(c - 2))
            tick("sD", e.dma_start(
                XT1[c % 2][:, :],
                x_d.rearrange("(c p) n -> p c n", p=128)[:, c, :]), 16)

    def p1_tensor(e):
        wait(e, "sV", sV0p1)
        for ci in range(2 * LC):
            c, half = ci // 2, ci % 2
            wait(e, "sV", V_cast1(c))
            if ci >= 2:
                wait(e, "sV", V_tr1(ci - 2))
            for k in range(4):
                kk = half * 4 + k
                tick("sT", e.transpose(ps_tr[ci % 2][:, k, :],
                                       XTM[:, c, kk * 128:(kk + 1) * 128],
                                       ident_b[:, :]))
        wait(e, "sV", V_cast1(LC - 1))
        for k in range(KC):
            if k >= 2:
                wait(e, "sV", V_xs1(k - 2))
            for c in range(LC):
                tick("sT", e.matmul(ps_xs[k % 2][:, :],
                                    XTM[:, c, k * 128:(k + 1) * 128],
                                    GsT[:, c, :],
                                    start=(c == 0), stop=(c == LC - 1)))

    def p1_vector(e):
        for c in range(LC):
            wait(e, "sD", D_x1(c))
            tick("sV", e.tensor_copy(XTM[:, c, :], XT1[c % 2][:, :]))
        for ci in range(2 * LC):
            c, half = ci // 2, ci % 2
            wait(e, "sT", T_tr1(ci))
            tick("sV", e.tensor_copy(
                XFM[:, half * 4:(half + 1) * 4, c * 128:(c + 1) * 128],
                ps_tr[ci % 2][:, :, :]))
        for k in range(KC):
            wait(e, "sT", T_xs1(k))
            tick("sV", e.tensor_copy(XS[:, k, :], ps_xs[k % 2][:, :]))

    blk.sync(p1_sync)
    blk.tensor(p1_tensor)
    blk.vector(p1_vector)

    def dump(pairs):
        def f(e):
            barrier(e)
            for name, src in pairs:
                if name in dbg:
                    tick("sD", e.dma_start(dbg[name], src), 16)
            wait(e, "sD")
        blk.sync(f)

        def w(e):
            wait(e, "sD")
        blk.tensor(w)
        blk.vector(w)
        blk.scalar(w)
        blk.gpsimd(w)

    def finish(*stacks):
        def fin(e):
            barrier(e)
        blk.sync(fin)
        for s in stacks:
            s.close()
        es.close()
        return nc

    if debug:
        dump([("xtm", XTM[:, :, :]), ("xfm", XFM[:, :, :]), ("xs", XS[:, :, :]), ("gst", GsT[:, :, :]), ("idxbc", idx_bc[:, :]), ("lcols", lcols[:, :]), ("idxf", idx_f[:, :]), ("borow1", bo_row[:, :])])
    if stop_after == "P1":
        return finish(p1, esW, esC, esB, esA)

    p1.close()

    WB = [sb_in(esW, "WB0", [128, KC, D], BF16, side="right"),
          sb_in(esW, "WB1", [128, KC, D], BF16, side="right")]
    p2 = ExitStack()
    WL2 = sb_in(p2, "WL2", [128, 2, D], F32, side="right")
    ps_pp = [psum_in(p2, f"ps_pp{i}", [128, 512]) for i in range(4)]
    ps_ks = [psum_in(p2, f"ps_ks{i}", [128, U]) for i in range(2)]
    ps_mv = psum_in(p2, "ps_mv", [1, 2, 512])

    sT0p2 = C["sT"]
    sV0p2 = C["sV"]
    sS0p2 = C["sS"]
    gD0p2 = C["gD"]
    TQ_END = sT0p2 + 32
    TKS_END = sT0p2 + 72

    def T_grp2(g):
        if g < 64:
            return sT0p2 + g + 1
        return sT0p2 + 8 + g + 1

    def T_ks2(hp):
        return sT0p2 + 64 + hp + 1

    def T_mv2(half):
        return sT0p2 + 104 + half + 1

    def V_grp2(g):
        if g < 64:
            return sV0p2 + g + 1
        return sV0p2 + 8 + g + 1

    def V_ks2(hp):
        return sV0p2 + 64 + hp + 1

    W_SRCS = [wq_d, wk_d, wv_d, wo_d]

    def S2_cast(w, k):
        return sS0p2 + 8 * w + k + 1

    def p2_gpsimd(e):
        barrier(e)
        for w in range(4):
            if w == 2:
                wait(e, "sT", TQ_END)
            if w == 3:
                wait(e, "sT", TKS_END)
            for k in range(KC):
                ci = 8 * w + k
                if ci >= 2:
                    wait(e, "sS", S2_cast((ci - 2) // 8, (ci - 2) % 8))
                tick("gD", e.dma_start(
                    WL2[:, ci % 2, :],
                    W_SRCS[w].rearrange("(k p) n -> p k n", p=128)[:, k, :]), 16)

    def p2_scalar(e):
        for w in range(4):
            for k in range(KC):
                ci = 8 * w + k
                wait(e, "gD", gD0p2 + _dma_even(ci + 1, 32))
                tick("sS", e.copy(WB[w % 2][:, k, :], WL2[:, ci % 2, :]))

    S_wq = sS0p2 + 8
    S_wk = sS0p2 + 16
    S_wv = sS0p2 + 24
    S_wo = sS0p2 + 32

    def p2_tensor(e):
        wait(e, "sS", S_wq)
        g = 0
        for c in range(LC):
            for half in range(2):
                if g >= 4:
                    wait(e, "sV", V_grp2(g - 4))
                for k in range(KC):
                    e.matmul(ps_pp[g % 4][:, :],
                             XFM[:, k, c * 128:(c + 1) * 128],
                             WB[0][:, k, half * 512:(half + 1) * 512],
                             start=(k == 0), stop=False)
                tick("sT", e.matmul(ps_pp[g % 4][:, :], ones_row_b[:, :],
                                    bq_row[0:1, half * 512:(half + 1) * 512],
                                    start=False, stop=True))
                g += 1
        wait(e, "sS", S_wk)
        for m in range(KC):
            for n in range(4):
                wait(e, "sV", V_grp2(g - 4))
                for k in range(KC):
                    instr = e.matmul(ps_pp[g % 4][:, :],
                                     WB[1][:, k, m * 128:(m + 1) * 128],
                                     XFM[:, k, n * 512:(n + 1) * 512],
                                     start=(k == 0), stop=(k == KC - 1))
                    if k == KC - 1:
                        tick("sT", instr)
                g += 1
        for hp in range(E):
            if hp >= 2:
                wait(e, "sV", V_ks2(hp - 2))
            for h01 in range(2):
                for k in range(KC):
                    instr = e.matmul(ps_ks[hp % 2][h01 * 64:(h01 + 1) * 64, :],
                                     WB[1][:, k, (2 * hp + h01) * 64:(2 * hp + h01 + 1) * 64],
                                     XS[:, k, :],
                                     start=(k == 0), stop=(k == KC - 1),
                                     tile_position=(0, h01 * 64))
                    if h01 == 1 and k == KC - 1:
                        tick("sT", instr)
        wait(e, "sS", S_wv)
        for c in range(LC):
            for half in range(2):
                wait(e, "sV", V_grp2(g - 4))
                for k in range(KC):
                    e.matmul(ps_pp[g % 4][:, :],
                             XFM[:, k, c * 128:(c + 1) * 128],
                             WB[0][:, k, half * 512:(half + 1) * 512],
                             start=(k == 0), stop=False)
                tick("sT", e.matmul(ps_pp[g % 4][:, :], ones_row_b[:, :],
                                    bv_row[0:1, half * 512:(half + 1) * 512],
                                    start=False, stop=True))
                g += 1
        wait(e, "sV", V_grp2(95))
        for half in range(2):
            for c in range(LC):
                instr = e.matmul(ps_mv[0:1, half, :], ones_col_b[:, :],
                                 VTM[:, c, half * 512:(half + 1) * 512],
                                 start=(c == 0), stop=(c == LC - 1))
                if c == LC - 1:
                    tick("sT", instr)

    def p2_vector(e):
        g = 0
        for c in range(LC):
            for half in range(2):
                wait(e, "sT", T_grp2(g))
                tick("sV", e.tensor_copy(QTM[:, c, half * 512:(half + 1) * 512],
                                         ps_pp[g % 4][:, :]))
                g += 1
        for m in range(KC):
            for n in range(4):
                wait(e, "sT", T_grp2(g))
                tick("sV", e.tensor_scalar(KFM[:, m, n * 512:(n + 1) * 512],
                                           ps_pp[g % 4][:, :], bk_cols[:, m:m + 1],
                                           None, ALU.add))
                g += 1
        for hp in range(E):
            wait(e, "sT", T_ks2(hp))
            tick("sV", e.tensor_scalar(KS[:, hp, :], ps_ks[hp % 2][:, :],
                                       bk_cols[:, hp:hp + 1], None, ALU.add))
        for c in range(LC):
            for half in range(2):
                wait(e, "sT", T_grp2(g))
                tick("sV", e.tensor_copy(VTM[:, c, half * 512:(half + 1) * 512],
                                         ps_pp[g % 4][:, :]))
                g += 1
        for half in range(2):
            wait(e, "sT", T_mv2(half))
            tick("sV", e.tensor_scalar(mv_row_f[0:1, half * 512:(half + 1) * 512],
                                       ps_mv[0:1, half, :], 1.0 / L, None,
                                       ALU.mult))
        vsp(e)
        tick("sV", e.tensor_copy(mv_row_b[:, :], mv_row_f[:, :]))

    blk.gpsimd(p2_gpsimd)
    blk.scalar(p2_scalar)
    blk.tensor(p2_tensor)
    blk.vector(p2_vector)
    p2.close()

    if debug:
        dump([("qtm", QTM[:, :, :]), ("kfm", KFM[:, :, :]),
              ("vtm", VTM[:, :, :]), ("ks", KS[:, :, :]),
              ("meanv", mv_row_f[:, :]), ("borow2", bo_row[:, :])])
    if stop_after == "P2":
        return finish(esW, esC, esB, esA)

    esC.close()
    esW.close()

    esR1 = ExitStack()
    CTX = sb_in(esR1, "CTX", [128, KC, L], BF16, side="right")
    esR2 = ExitStack()
    MPACK = sb_in(esR2, "MPACK", [128, H, LC], F32, side="right")
    TAUBC = sb_in(esR2, "TAUBC", [128, H], F32, side="right")
    SCNA = sb_in(esR2, "SCNA", [128, H, LC], F32, side="right")
    p3 = ExitStack()
    QF2 = [sb_in(p3, f"QF2_{i}", [128, LC, 128], BF16) for i in range(2)]
    MXA = sb_in(p3, "MXA", [128, H, LC], F32)
    NSA = sb_in(p3, "NSA", [128, H, LC], F32)
    MHM = sb_in(p3, "MHM", [16, L], F32)
    MMUT = sb_in(p3, "MMUT", [16, L], F32)
    MAXV = sb_in(p3, "MAXV", [16, 48], F32)
    T01 = sb_in(p3, "T01", [128, 2, 128], F32)
    TAUR = sb_in(p3, "TAUR", [1, 16], F32)
    SCN = sb_in(p3, "SCN", [128, 2, LC], F32)
    TOTB = sb_in(p3, "TOTB", [128, H], BF16)
    OFFS = sb_in(p3, "OFFS", [128, H], F32)
    MRED = sb_in(p3, "MRED", [128, 2, 8], F32)

    p3a = ExitStack()
    ps_qf = [psum_in(p3a, f"ps_qf{i}", [128, 4, 128], BF16) for i in range(2)]
    ps_qk = [psum_in(p3a, f"ps_qk{i}", [128, 8, U]) for i in range(4)]

    sT0p3 = C["sT"]
    sV0p3 = C["sV"]

    def T_qf3(gi):
        return sT0p3 + 8 * (gi // 4) + (gi % 4) + 1

    def T_qk3(qi):
        return sT0p3 + 8 * (qi // 4) + 4 + (qi % 4) + 1

    def V_qf3(gi):
        return sV0p3 + 8 * (gi // 4) + (gi % 4) + 1

    def V_mp3(qi):
        return sV0p3 + 8 * (qi // 4) + 4 + (qi % 4) + 1

    P3_SERIAL = True

    def p3_tensor(e):
        barrier(e)
        gi = 0
        qi = 0
        for hp in range(E):
            for c4 in range(4):
                if P3_SERIAL and (gi > 0 or hp > 0):
                    wait(e, "sV", V_qf3(gi - 1) if gi > 0 else 0)
                    if qi > 0:
                        wait(e, "sV", V_mp3(qi - 1))
                elif gi >= 2:
                    wait(e, "sV", V_qf3(gi - 2))
                for k in range(4):
                    c = c4 * 4 + k
                    instr = e.transpose(ps_qf[gi % 2][:, k, :],
                                        QTM[:, c, hp * 128:(hp + 1) * 128],
                                        ident_b[:, :])
                    if k == 3:
                        tick("sT", instr)
                gi += 1
            wait(e, "sV", V_qf3(4 * hp + 3))
            for h01 in range(2):
                for chalf in range(2):
                    if P3_SERIAL and qi > 0:
                        wait(e, "sV", V_mp3(qi - 1))
                    elif qi >= 4:
                        wait(e, "sV", V_mp3(qi - 4))
                    for c8 in range(8):
                        c = chalf * 8 + c8
                        instr = e.matmul(
                            ps_qk[qi % 4][:, c8, :],
                            QF2[hp % 2][h01 * 64:(h01 + 1) * 64, c, :],
                            KS[h01 * 64:(h01 + 1) * 64, hp, :],
                            start=True, stop=True,
                            tile_position=(h01 * 64, 0))
                        if c8 == 7:
                            tick("sT", instr)
                    qi += 1

    def p3_vector(e):
        gi = 0
        qi = 0
        for hp in range(E):
            for c4 in range(4):
                wait(e, "sT", T_qf3(gi))
                tick("sV", e.tensor_copy(QF2[hp % 2][:, c4 * 4:(c4 + 1) * 4, :],
                                         ps_qf[gi % 2][:, :, :]))
                gi += 1
            for h01 in range(2):
                h = 2 * hp + h01
                for chalf in range(2):
                    wait(e, "sT", T_qk3(qi))
                    e.reduce_max(MXA[:, h, chalf * 8:(chalf + 1) * 8],
                                 ps_qk[qi % 4][:, :, :], axis=AX.X)
                    tick("sV", e.reduce_sum(NSA[:, h, chalf * 8:(chalf + 1) * 8],
                                            ps_qk[qi % 4][:, :, :],
                                            axis=AX.X, negate=True))
                    qi += 1
        vsp(e)
        tick("sV", e.scalar_tensor_tensor(MPACK[:, :, :], NSA[:, :, :], 1.0 / U,
                                          MXA[:, :, :], ALU.mult, ALU.add))

    blk.tensor(p3_tensor)
    blk.vector(p3_vector)
    p3a.close()
    ps_mt = psum_in(p3, "ps_mt", [128, 128])
    ps_off = psum_in(p3, "ps_off", [128, H])

    def mk_t(fn):
        blk.tensor(fn)

    def mk_v(fn):
        blk.vector(fn)

    def mk_g(fn):
        blk.gpsimd(fn)

    def t_half(half):
        def f(e):
            wait(e, "sV")
            m2d = MPACK[:, :, :].rearrange("p h c -> p (h c)")
            tick("sT", e.transpose(ps_mt[:, :], m2d[:, half * 128:(half + 1) * 128],
                                   ident_f[:, :]))
        return f

    def v_half(half):
        def f(e):
            wait(e, "sT")
            tick("sV", e.tensor_copy(T01[:, half, :], ps_mt[:, :]))
        return f

    mk_t(t_half(0))
    mk_v(v_half(0))
    mk_t(t_half(1))
    mk_v(v_half(1))

    def g_flat(e):
        wait(e, "sV")
        tick("gD", e.dma_start(MHM[0:8, :], T01[:, 0, :]), 16)
        tick("gD", e.dma_start(MHM[8:16, :], T01[:, 1, :]), 16)
    mk_g(g_flat)

    def v_topk(e):
        wait(e, "gD")
        tick("sV", e.tensor_copy(MMUT[:, :], MHM[:, :]))
        for r in range(5):
            sl = slice(r * 8, r * 8 + 8)
            vsp(e)
            tick("sV", e.max(MAXV[:, sl], MMUT[:, :]))
            vsp(e)
            tick("sV", e.match_replace(MMUT[:, :], MAXV[:, sl], MMUT[:, :], -1e30))
        vsp(e)
        tick("sV", e.max(MAXV[:, 40:48], MMUT[:, :]))
    mk_v(v_topk)

    def t_taur(e):
        wait(e, "sV")
        tick("sT", e.transpose(ps_mt[0:1, 0:16], MAXV[:, 40:41],
                               ident_f[0:16, 0:16]))
    mk_t(t_taur)

    def v_taur(e):
        wait(e, "sT")
        tick("sV", e.tensor_copy(TAUR[0:1, :], ps_mt[0:1, 0:16]))
    mk_v(v_taur)

    def t_taubc(e):
        wait(e, "sV")
        tick("sT", e.matmul(ps_mt[:, 0:16], ones_row_f[:, :], TAUR[0:1, :],
                            start=True, stop=True))
    mk_t(t_taubc)

    def v_taubc(e):
        wait(e, "sT")
        tick("sV", e.tensor_copy(TAUBC[:, :], ps_mt[:, 0:16]))
    mk_v(v_taubc)

    MSKA = sb_in(p3, "MSKA", [128, H, LC], F32)

    def v_rank_a2(e):
        vsp(e)
        for h in range(H):
            tick("sV", e.tensor_scalar(MSKA[:, h, :], MPACK[:, h, :],
                                       TAUBC[:, h:h + 1], None, ALU.is_ge))
        vsp(e)
        for h in range(H):
            tick("sV", e.tensor_tensor_scan(SCNA[:, h, :], MSKA[:, h, :],
                                            zeros16[:, :], 0.0, ALU.add, ALU.add))
        vsp(e)
        for h in range(H):
            tick("sV", e.tensor_copy(TOTB[:, h:h + 1], SCNA[:, h, LC - 1:LC]))
    mk_v(v_rank_a2)

    def t_offs(e):
        wait(e, "sV")
        for h in range(H):
            instr = e.matmul(ps_off[:, h:h + 1], tri_b[:, :], TOTB[:, h:h + 1],
                             start=True, stop=True)
            if h == H - 1:
                tick("sT", instr)
    mk_t(t_offs)

    def v_rank_b(e):
        wait(e, "sT")
        tick("sV", e.tensor_copy(OFFS[:, :], ps_off[:, :]))
        vsp(e)
        for h in range(H):
            tick("sV", e.tensor_scalar(SCNA[:, h, :], SCNA[:, h, :],
                                       OFFS[:, h:h + 1], None, ALU.add))
        vsp(e)
        for h in range(H):
            tick("sV", e.tensor_mul(SCNA[:, h, :], SCNA[:, h, :], MSKA[:, h, :]))
    mk_v(v_rank_b)

    if debug:
        dump([("mpack", MPACK[:, :, :]), ("tau", MAXV[:, :]),
              ("rstar", SCNA[:, :, :]), ("qf2", QF2[1][:, :, :]), ("mhm", MHM[:, :]),
              ("ks2", KS[:, :, :]), ("borow3", bo_row[:, :])])
    if stop_after == "P3":
        return finish(p3, esR2, esR1, esB, esA)

    p3.close()
    esB.close()

    p4 = ExitStack()
    GAUG = [sb_in(p4, f"GAUG{i}", [128, L], BF16) for i in range(2)]
    DUDBG = sb_in(p4, "DUDBG", [128, DK], F32)
    GTA4 = [sb_in(p4, f"GTA4_{i}", [128, 2, LC, 42], BF16) for i in range(2)]
    QST = [sb_in(p4, f"QST{i}", [128, U], BF16) for i in range(2)]
    PT = [sb_in(p4, f"PT{i}", [128, 2, LC, U], BF16) for i in range(2)]
    DA = [sb_in(p4, f"DA{i}", [128, DK], BF16) for i in range(2)]
    RC4 = [sb_in(p4, f"RC4_{i}", [128, 1], F32) for i in range(2)]

    ps_ga = [psum_in(p4, f"ps_ga{i}", [128, 4, 128], BF16) for i in range(2)]
    ps_qs = psum_in(p4, "ps_qs", [128, U])
    ps_st = [psum_in(p4, f"ps_st{i}", [128, 8, U]) for i in range(2)]
    ps_misc = psum_in(p4, "ps_misc", [128, 128])
    ps_cx = [psum_in(p4, f"ps_cx{i}", [128, 512]) for i in range(2)]

    T0p4 = C["sT"]
    V0p4 = C["sV"]
    S0p4 = C["sS"]
    TPP = 19
    VPP = 18
    SPP = 4

    def p4_gpsimd(e):
        barrier(e)
        tick("sG", e.memset(RC4[0][:, :], 1.0))
        tick("sG", e.memset(RC4[1][:, :], 1.0))

    def T4_ga(hp, i):
        return T0p4 + TPP * hp + i + 1

    def T4_qs(hp):
        return T0p4 + TPP * hp + 9

    def T4_st(hp, s):
        return T0p4 + TPP * hp + 10 + s

    def T4_rs(hp):
        return T0p4 + TPP * hp + 14

    def T4_du(hp):
        return T0p4 + TPP * hp + 15

    def T4_cx(hp, n):
        return T0p4 + TPP * hp + 16 + n

    def V4_gt(hp):
        return V0p4 + VPP * hp + 1

    def V4_ga(hp, i):
        return V0p4 + VPP * hp + 2 + i

    def V4_qs(hp):
        return V0p4 + VPP * hp + 10

    def V4_rcp(hp):
        return V0p4 + VPP * hp + 12

    def V4_da(hp):
        return V0p4 + VPP * hp + 14

    def V4_cx(hp, n):
        return V0p4 + VPP * hp + 15 + n

    def S4_st(hp, s):
        return S0p4 + SPP * hp + s + 1

    def p4_tensor(e):
        barrier(e)
        for hp in range(E):
            par = hp % 2
            h0, h1 = 2 * hp, 2 * hp + 1
            wait(e, "sV", V4_gt(hp))
            for h01 in range(2):
                for nb in range(4):
                    i = h01 * 4 + nb
                    bi = 8 * hp + i
                    if bi >= 2:
                        wait(e, "sV", V4_ga((bi - 2) // 8, (bi - 2) % 8))
                    for k in range(4):
                        c = nb * 4 + k
                        instr = e.transpose(
                            ps_ga[bi % 2][h01 * 64:h01 * 64 + 42, k, :],
                            GTA4[par][:, h01, c, :],
                            ident_b[:, :],
                            tile_position=(0, h01 * 64))
                        if k == 3:
                            tick("sT", instr)
            if hp >= 1:
                wait(e, "sV", V4_qs(hp - 1))
            for h01, h in ((0, h0), (1, h1)):
                for c in range(LC):
                    instr = e.matmul(ps_qs[h01 * 64:(h01 + 1) * 64, :],
                                     QTM[:, c, h * 64:(h + 1) * 64],
                                     GTA4[par][:, h01, c, 0:U],
                                     start=(c == 0), stop=(c == LC - 1),
                                     tile_position=(0, h01 * 64))
                    if h01 == 1 and c == LC - 1:
                        tick("sT", instr)
            wait(e, "sV", V4_qs(hp))
            for h01 in range(2):
                for chalf in range(2):
                    s = 2 * h01 + chalf
                    sg = 4 * hp + s
                    if sg >= 2:
                        wait(e, "sS", S4_st((sg - 2) // 4, (sg - 2) % 4))
                    for c8 in range(8):
                        c = chalf * 8 + c8
                        instr = e.matmul(
                            ps_st[sg % 2][:, c8, :],
                            KFM[h01 * 64:(h01 + 1) * 64, hp, c * 128:(c + 1) * 128],
                            QST[par][h01 * 64:(h01 + 1) * 64, :],
                            start=True, stop=True,
                            tile_position=(h01 * 64, 0))
                        if c8 == 7:
                            tick("sT", instr)
            wait(e, "sS", S4_st(hp, 3))
            if hp >= 1:
                wait(e, "sV", V4_da(hp - 1))
            for h01 in range(2):
                for c in range(LC):
                    instr = e.matmul(ps_misc[h01 * 64:h01 * 64 + U, 124:125],
                                     PT[par][:, h01, c, :],
                                     ones_col_b[:, :],
                                     start=(c == 0), stop=(c == LC - 1),
                                     tile_position=(0, h01 * 64))
                    if h01 == 1 and c == LC - 1:
                        tick("sT", instr)
            wait(e, "sV", V4_rcp(hp))
            for h01, h in ((0, h0), (1, h1)):
                e.matmul(ps_misc[h01 * 64:h01 * 64 + 42, 0:DK],
                         e42_row[:, :], mv_row_b[0:1, h * 64:(h + 1) * 64],
                         start=True, stop=False,
                         tile_position=(0, h01 * 64))
                for c in range(LC):
                    e.matmul(ps_misc[h01 * 64:h01 * 64 + U, 0:DK],
                             PT[par][:, h01, c, :],
                             VTM[:, c, h * 64:(h + 1) * 64],
                             start=False, stop=False,
                             tile_position=(0, h01 * 64))
                instr = e.matmul(ps_misc[h01 * 64:h01 * 64 + 42, 0:DK],
                                 zrow42[:, :], mv_row_b[0:1, h * 64:(h + 1) * 64],
                                 start=False, stop=True,
                                 tile_position=(0, h01 * 64))
                if h01 == 1:
                    tick("sT", instr)
            wait(e, "sV", V4_da(hp))
            wait(e, "sV", V4_ga(hp, 7))
            for n in range(4):
                b = 4 * hp + n
                if b >= 2:
                    wait(e, "sV", V4_cx((b - 2) // 4, (b - 2) % 4))
                e.matmul(ps_cx[n % 2][0:64, :], DA[par][0:42, :],
                         GAUG[par][0:42, n * 512:(n + 1) * 512],
                         start=True, stop=True)
                tick("sT", e.matmul(ps_cx[n % 2][64:128, :], DA[par][64:106, :],
                                    GAUG[par][64:106, n * 512:(n + 1) * 512],
                                    start=True, stop=True,
                                    tile_position=(64, 64)))

    def p4_scalar(e):
        for hp in range(E):
            par = hp % 2
            for h01 in range(2):
                for chalf in range(2):
                    s = 2 * h01 + chalf
                    wait(e, "sT", T4_st(hp, s))
                    tick("sS", e.activation(
                        PT[par][:, h01, chalf * 8:(chalf + 1) * 8, :],
                        ps_st[(4 * hp + s) % 2][:, :, :], AF.Exp,
                        bias=0.0, scale=1.0 / 8.0))

    def p4_vector(e):
        vsp(e)
        for hp in range(E):
            par = hp % 2
            h0, h1 = 2 * hp, 2 * hp + 1
            if hp >= 2:
                wait(e, "sT", T4_qs(hp - 2))
            for h01, h in ((0, h0), (1, h1)):
                for c in range(LC):
                    instr = e.tensor_scalar(GTA4[par][:, h01, c, :],
                                            u42_bc[:, :],
                                            SCNA[:, h, c:c + 1], None,
                                            ALU.is_equal)
                    if h01 == 1 and c == LC - 1:
                        tick("sV", instr)
            for h01 in range(2):
                for nb in range(4):
                    i = h01 * 4 + nb
                    bi = 8 * hp + i
                    wait(e, "sT", T4_ga(hp, i))
                    tick("sV", e.tensor_copy(
                        GAUG[par][h01 * 64:h01 * 64 + 42,
                                  nb * 512:(nb + 1) * 512],
                        ps_ga[bi % 2][h01 * 64:h01 * 64 + 42, :, :]))
            wait(e, "sT", T4_qs(hp))
            tick("sV", e.tensor_copy(QST[par][:, :], ps_qs[:, :]))
            wait(e, "sT", T4_rs(hp))
            tick("sV", e.reciprocal(RC4[par][0:U, :], ps_misc[0:U, 124:125]))
            tick("sV", e.reciprocal(RC4[par][64:64 + U, :],
                                    ps_misc[64:64 + U, 124:125]))
            vsp(e)
            wait(e, "sT", T4_du(hp))
            if hp == 7 and debug:
                e.tensor_copy(DUDBG[0:42, :], ps_misc[0:42, 0:DK])
                e.tensor_copy(DUDBG[64:106, :], ps_misc[64:106, 0:DK])
            tick("sV", e.tensor_scalar(DA[par][0:42, :], ps_misc[0:42, 0:DK],
                                       RC4[par][0:42, :], None, ALU.mult))
            tick("sV", e.tensor_scalar(DA[par][64:106, :],
                                       ps_misc[64:106, 0:DK],
                                       RC4[par][64:106, :], None, ALU.mult))
            for n in range(4):
                wait(e, "sT", T4_cx(hp, n))
                tick("sV", e.tensor_copy(CTX[:, hp, n * 512:(n + 1) * 512],
                                         ps_cx[n % 2][:, :]))

    blk.gpsimd(p4_gpsimd)
    blk.tensor(p4_tensor)
    blk.scalar(p4_scalar)
    blk.vector(p4_vector)
    p4.close()
    esR2.close()
    esA.close()

    if debug:
        dump([("ctx", CTX[:, :, :]), ("qst7", QST[1][:, :]), ("pt7", PT[1][:, :, :, :]),
              ("rc7", RC4[1][:, :]), ("da7", DA[1][:, :]), ("gaug7", GAUG[1][:, :]),
              ("gta7", GTA4[1][:, :, :, :]), ("e42", e42_row[:, :]),
              ("mvb", mv_row_b[:, :]), ("duraw", DUDBG[:, :]),
              ("borow4", bo_row[:, :])])
    if stop_after == "P4":
        return finish(esR1)

    A = sb_in(es, "A", [128, KC, L], F32)
    X1B = sb_in(es, "X1B", [128, KC, L], BF16)

    p5 = ExitStack()
    WO = sb_in(p5, "WO", [128, KC, D], BF16)
    WL5 = sb_in(p5, "WL5", [128, 2, D], F32)
    XT5 = [sb_in(p5, f"XT5_{i}", [128, D], F32) for i in range(2)]
    ST5 = sb_in(p5, "ST5", [128, 8], F32)
    LOGIT = sb_in(p5, "LOGIT", [128, LC, E], F32)
    GWRK = sb_in(p5, "GWRK", [128, 3, LC, E], F32)
    NGR = sb_in(p5, "NGR", [16, 128], F32)
    NGROW = sb_in(p5, "NGROW", [1, L], F32)
    RGB = sb_in(p5, "RGB", [128, L], F32)

    ps_o = [psum_in(p5, f"ps_o{i}", [128, 512]) for i in range(2)]
    ps_x1 = [psum_in(p5, f"ps_x1{i}", [128, 4, 128]) for i in range(2)]
    ps_gt = [psum_in(p5, f"ps_gt{i}", [128, E]) for i in range(2)]
    ps_ng = psum_in(p5, "ps_ng", [128, 128])
    ps_sq5 = psum_in(p5, "ps_sq5", [128, 512])

    P5S = dict(C)

    def T5_o(c, half):
        if c == 0:
            return P5S["sT"] + half + 1
        return P5S["sT"] + 5 * c + half - 2

    def T5_x1(c, kb):
        if c == LC - 1:
            return P5S["sT"] + 78 + kb
        return P5S["sT"] + 5 * c + 5 + kb

    def T5_g(c):
        if c == LC - 1:
            return P5S["sT"] + 80
        return P5S["sT"] + 5 * c + 7

    def V5_oev(c, half):
        return P5S["sV"] + 15 * c + half + 1

    def V5_r(c):
        return P5S["sV"] + 15 * c + 2

    def V5_var(c):
        return P5S["sV"] + 15 * c + 8

    def V5_app(c):
        return P5S["sV"] + 15 * c + 10

    def V5_x1ev(c, kb):
        return P5S["sV"] + 15 * c + 12 + 2 * kb

    def V5_g(c):
        return P5S["sV"] + 15 * c + 15

    def S5_cast(i):
        return P5S["sS"] + i + 1

    def S5_sq(c):
        return P5S["sS"] + 8 + 3 * c + 2

    def S5_sd(c):
        return P5S["sS"] + 8 + 3 * c + 3

    def D5_xt(c):
        return P5S["sD"] + _dma_even(c + 1, LC)

    def p5_gpsimd(e):
        barrier(e)
        for i in range(KC):
            if i >= 2:
                wait(e, "sS", S5_cast(i - 2))
            tick("gD", e.dma_start(
                WL5[:, i % 2, :],
                wo_d.rearrange("(k p) n -> p k n", p=128)[:, i, :]), 16)

    def p5_sync(e):
        barrier(e)
        for c in range(LC):
            if c >= 2:
                wait(e, "sT", T5_x1(c - 2, 1))
            tick("sD", e.dma_start(
                XT5[c % 2][:, :],
                x_d.rearrange("(c p) n -> p c n", p=128)[:, c, :]), 16)

    def p5_tensor(e):
        barrier(e, P5S)
        wait(e, "sS", P5S["sS"] + 8)

        def oproj(c):
            for half in range(2):
                key = 2 * c + half
                if key >= 2:
                    wait(e, "sV", V5_oev((key - 2) // 2, (key - 2) % 2))
                for k in range(KC):
                    e.matmul(ps_o[key % 2][:, :],
                             CTX[:, k, c * 128:(c + 1) * 128],
                             WO[:, k, half * 512:(half + 1) * 512],
                             start=(k == 0), stop=False)
                tick("sT", e.matmul(ps_o[key % 2][:, :], ones_row_b[:, :],
                                    bo_row[0:1, half * 512:(half + 1) * 512],
                                    start=False, stop=True))

        def lnpart(c):
            par = c % 2
            wait(e, "sV", V5_app(c))
            for kb in range(2):
                if c >= 1:
                    wait(e, "sV", V5_x1ev(c - 1, kb))
                for k in range(4):
                    kk = kb * 4 + k
                    instr = e.transpose(ps_x1[kb][:, k, :],
                                        XT5[par][:, kk * 128:(kk + 1) * 128],
                                        ident_f[:, :])
                    if k == 3:
                        tick("sT", instr)
            wait(e, "sV", V5_x1ev(c, 1))
            if c >= 2:
                wait(e, "sV", V5_g(c - 2))
            for k in range(KC):
                instr = e.matmul(ps_gt[c % 2][:, :],
                                 X1B[:, k, c * 128:(c + 1) * 128],
                                 gate_wT[:, k, :],
                                 start=(k == 0), stop=(k == KC - 1))
                if k == KC - 1:
                    tick("sT", instr)

        oproj(0)
        for c in range(1, LC):
            oproj(c)
            lnpart(c - 1)
        lnpart(LC - 1)

    def p5_scalar(e):
        for i in range(KC):
            wait(e, "gD", P5S["gD"] + _dma_even(i + 1, KC))
            tick("sS", e.copy(WO[:, i, :], WL5[:, i % 2, :]))
        for c in range(LC):
            par = c % 2
            wait(e, "sV", V5_r(c))
            tick("sS", e.activation(ps_sq5[:, :], XT5[par][:, 0:512], AF.Square,
                                    bias=0.0, scale=1.0,
                                    accum_out=ST5[:, 6:7]))
            wait(e, "sS")
            tick("sS", e.activation(ps_sq5[:, :], XT5[par][:, 512:1024], AF.Square,
                                    bias=0.0, scale=1.0,
                                    accum_out=ST5[:, 7:8]))
            wait(e, "sV", V5_var(c))
            tick("sS", e.activation(ST5[:, 4:5], ST5[:, 3:4], AF.Sqrt,
                                    bias=eps_col[:, :], scale=1.0))

    def p5_vector(e):
        for c in range(LC):
            par = c % 2
            for half in range(2):
                key = 2 * c + half
                wait(e, "sT", T5_o(c, half))
                wait(e, "sD", D5_xt(c))
                if c >= 2:
                    wait(e, "sS", S5_sq(c - 2))
                tick("sV", e.tensor_add(XT5[par][:, half * 512:(half + 1) * 512],
                                        ps_o[key % 2][:, :],
                                        XT5[par][:, half * 512:(half + 1) * 512]))
            vsp(e)
            tick("sV", e.reduce_sum(ST5[:, 0:1], XT5[par][:, :], axis=AX.X))
            wait(e, "sS", S5_sq(c))
            tick("sV", e.tensor_add(ST5[:, 1:2], ST5[:, 6:7], ST5[:, 7:8]))
            vsp(e)
            tick("sV", e.tensor_single_scalar(ST5[:, 0:1], ST5[:, 0:1], 1.0 / D,
                                              ALU.mult))
            vsp(e)
            tick("sV", e.tensor_single_scalar(ST5[:, 1:2], ST5[:, 1:2], 1.0 / D,
                                              ALU.mult))
            vsp(e)
            tick("sV", e.tensor_mul(ST5[:, 2:3], ST5[:, 0:1], ST5[:, 0:1]))
            vsp(e)
            tick("sV", e.tensor_sub(ST5[:, 3:4], ST5[:, 1:2], ST5[:, 2:3]))
            wait(e, "sS", S5_sd(c))
            tick("sV", e.reciprocal(ST5[:, 5:6], ST5[:, 4:5]))
            vsp(e)
            tick("sV", e.tensor_scalar(XT5[par][:, :], XT5[par][:, :],
                                       ST5[:, 0:1], ST5[:, 5:6],
                                       ALU.subtract, ALU.mult))
            for kb in range(2):
                wait(e, "sT", T5_x1(c, kb))
                tick("sV", e.tensor_copy(
                    A[:, kb * 4:(kb + 1) * 4, c * 128:(c + 1) * 128],
                    ps_x1[kb][:, :, :]))
                tick("sV", e.tensor_copy(
                    X1B[:, kb * 4:(kb + 1) * 4, c * 128:(c + 1) * 128],
                    ps_x1[kb][:, :, :]))
            wait(e, "sT", T5_g(c))
            tick("sV", e.tensor_add(LOGIT[:, c, :], ps_gt[c % 2][:, :],
                                    gate_b_bc[:, :]))

    blk.gpsimd(p5_gpsimd)
    blk.sync(p5_sync)
    blk.tensor(p5_tensor)
    blk.scalar(p5_scalar)
    blk.vector(p5_vector)

    def v_gate1(e):
        wait(e, "sV")
        wait(e, "sS")
        tick("sV", e.tensor_mul(GWRK[:, 0, :, :], LOGIT[:, :, :], LOGIT[:, :, :]))
        vsp(e)
        tick("sV", e.reduce_sum(NG[:, :], GWRK[:, 0, :, :], axis=AX.X))
        vsp(e)
        tick("sV", e.tensor_scalar(NG[:, :], NG[:, :], 1.0 / E, EPS_RMS,
                                   ALU.mult, ALU.add))
    blk.vector(v_gate1)

    def s_gate1(e):
        wait(e, "sV")
        tick("sS", e.activation(NG[:, :], NG[:, :], AF.Sqrt, bias=0.0, scale=1.0))
    blk.scalar(s_gate1)

    def v_gate2(e):
        wait(e, "sS")
        tick("sV", e.reciprocal(NG[:, :], NG[:, :]))
        vsp(e)
        tick("sV", e.tensor_mul(GWRK[:, 0, :, :], LOGIT[:, :, :],
                                NG[:, :, None].broadcast_to([128, LC, E])))
        vsp(e)
        tick("sV", e.tensor_mul(GWRK[:, 0, :, :], GWRK[:, 0, :, :],
                                rms_w_bc[:, None, :].broadcast_to([128, LC, E])))
        vsp(e)
        tick("sV", e.reduce_max(NG[:, :], GWRK[:, 0, :, :], axis=AX.X))
        vsp(e)
        tick("sV", e.tensor_sub(GWRK[:, 1, :, :], GWRK[:, 0, :, :],
                                NG[:, :, None].broadcast_to([128, LC, E])))
    blk.vector(v_gate2)

    def s_gate2(e):
        wait(e, "sV")
        tick("sS", e.activation(GWRK[:, 2, :, :], GWRK[:, 1, :, :], AF.Exp,
                                bias=0.0, scale=2.0))
    blk.scalar(s_gate2)

    def v_gate3(e):
        wait(e, "sS")
        tick("sV", e.reduce_sum(NG[:, :], GWRK[:, 2, :, :], axis=AX.X))
        vsp(e)
        tick("sV", e.tensor_scalar(NG[:, :], NG[:, :], 1e-6, 1.0,
                                   ALU.mult, ALU.add))
        vsp(e)
        tick("sV", e.reciprocal(NG[:, :], NG[:, :]))
    blk.vector(v_gate3)

    def t_gate(e):
        wait(e, "sV")
        tick("sT", e.transpose(ps_ng[0:16, :], NG[:, :], ident_f[:, :]))
    blk.tensor(t_gate)

    def v_gate4(e):
        wait(e, "sT")
        tick("sV", e.tensor_copy(NGR[:, :], ps_ng[0:16, :]))
    blk.vector(v_gate4)

    def g_gate(e):
        wait(e, "sV")
        tick("gD", e.dma_start(NGROW[0:1, :], NGR[:, :]), 16)
    blk.gpsimd(g_gate)

    def v_gate5(e):
        wait(e, "gD")
        tick("sV", e.reciprocal(NGROW[0:1, :], NGROW[0:1, :]))
    blk.vector(v_gate5)

    for _n in range(4):
        def t_bc(e, n=_n):
            wait(e, "sV")
            tick("sT", e.matmul(ps_o[n % 2][:, :], ones_row_f[:, :],
                                NGROW[0:1, n * 512:(n + 1) * 512],
                                start=True, stop=True))
        blk.tensor(t_bc)

        def v_bc(e, n=_n):
            wait(e, "sT")
            tick("sV", e.tensor_copy(RGB[:, n * 512:(n + 1) * 512],
                                     ps_o[n % 2][:, :]))
        blk.vector(v_bc)

    def v_ascale(e):
        vsp(e)
        for k in range(KC):
            tick("sV", e.tensor_mul(A[:, k, :], A[:, k, :], RGB[:, :]))
    blk.vector(v_ascale)

    if debug:
        dump([("x1b", X1B[:, :, :]), ("ng", NG[:, :]),
              ("logit", LOGIT[:, :, :]), ("borow5", bo_row[:, :])])
    if stop_after == "P5":
        return finish(p5, esR1)

    p5.close()
    esR1.close()

    p6 = ExitStack()
    WF = [sb_in(p6, f"WF{i}", [128, 16, D], BF16) for i in range(2)]
    WL6 = sb_in(p6, "WL6", [128, 2, D], F32)
    HSB = sb_in(p6, "HSB", [128, 2, KC, TF], BF16, side="right")
    ps_h = [psum_in(p6, f"ps_h{i}", [128, TF]) for i in range(2)]
    ps_y = psum_in(p6, "ps_y", [128, 4, TF])

    P6S = dict(C)
    TPH = NTF * 16
    S0_6 = P6S["sS"] + 8
    V0_6 = P6S["sV"] + 8

    def G6c(he, i):
        return P6S["gD"] + _dma_even(16 * he + i + 1, 16 * NHE)

    B1_6 = [0, 1, 3, 5]
    B2_6 = [2, 4, 6, 7]

    def T6_mm1(he, t, m):
        return P6S["sT"] + he * TPH + B1_6[t] * 8 + m + 1

    def T6_mm2(he, t, ps, j):
        return P6S["sT"] + he * TPH + B2_6[t] * 8 + ps * 4 + j + 1

    def S6_cast(he, j):
        if he == 0:
            return P6S["sS"] + j + 1
        return S0_6 + 40 * (he - 1) + 10 + 2 * j

    def S6(he, t, m):
        base = S0_6 + 40 * he
        last = (he == NHE - 1)
        if t == 0:
            return base + 1 + m
        if t == 1:
            return base + 9 + (m if last else 2 * m)
        if t == 2:
            return base + (17 if last else 25) + m
        return base + (25 if last else 33) + m

    def V6_cast(he, j):
        if he == 0:
            return P6S["sV"] + j + 1
        return V0_6 + 40 * (he - 1) + 18 + 2 * j

    def V6(he, t, ps, j):
        a = ps * 4 + j
        base = V0_6 + 40 * he
        last = (he == NHE - 1)
        if t == 0:
            return base + 1 + a
        if t == 1:
            return base + 9 + a
        if t == 2:
            return base + 17 + (a if last else 2 * a)
        return base + (25 if last else 33) + a

    def p6_gpsimd(e):
        barrier(e)
        for he in range(NHE):
            ex, half = he // 2, he % 2
            if he >= 2:
                wait(e, "sT", P6S["sT"] + (he - 1) * TPH)
            w1v = w1_d[ex, :, half * 1024:(half + 1) * 1024] \
                .rearrange("(k p) n -> p k n", p=128)
            w2v = w2_d[ex, half * 1024:(half + 1) * 1024, :] \
                .rearrange("(k p) n -> p k n", p=128)
            for i in range(16):
                if 16 * he + i >= 2:
                    gi = 16 * he + i - 2
                    he2, i2 = divmod(gi, 16)
                    if i2 < 8:
                        wait(e, "sS", S6_cast(he2, i2))
                    else:
                        wait(e, "sV", V6_cast(he2, i2 - 8))
                srcv = w1v[:, i, :] if i < 8 else w2v[:, i - 8, :]
                tick("gD", e.dma_start(WL6[:, i % 2, :], srcv), 16)

    def p6_tensor(e):
        for name in ["sV", "sS", "sD"]:
            wait(e, name, P6S[name])
        for he in range(NHE):
            wait(e, "sS", S6_cast(he, 7))

            def mm1_block(t):
                for m in range(KC):
                    q = (he * NTF + t) * 8 + m
                    if q >= 2:
                        q2 = q - 2
                        he2, r2 = divmod(q2, NTF * 8)
                        t2, m2 = divmod(r2, 8)
                        wait(e, "sS", S6(he2, t2, m2))
                    for k in range(KC):
                        instr = e.matmul(ps_h[q % 2][:, :],
                                         WF[he % 2][:, k, m * 128:(m + 1) * 128],
                                         X1B[:, k, t * TF:(t + 1) * TF],
                                         start=(k == 0), stop=(k == KC - 1))
                        if k == KC - 1:
                            tick("sT", instr)

            def mm2_block(t):
                wait(e, "sS", S6(he, t, 7))
                if t == 0:
                    wait(e, "sV", V6_cast(he, 7))
                for ps in range(2):
                    for j in range(4):
                        r = (he * NTF + t) * 2 + ps
                        if r >= 1:
                            r2 = r - 1
                            het2, ps2 = divmod(r2, 2)
                            he2, t2 = divmod(het2, NTF)
                            wait(e, "sV", V6(he2, t2, ps2, j))
                        for hk in range(KC):
                            instr = e.matmul(
                                ps_y[:, j, :],
                                WF[he % 2][:, 8 + hk,
                                           (ps * 4 + j) * 128:(ps * 4 + j + 1) * 128],
                                HSB[:, t % 2, hk, :],
                                start=(hk == 0), stop=(hk == KC - 1))
                            if hk == KC - 1:
                                tick("sT", instr)

            mm1_block(0)
            mm1_block(1)
            mm2_block(0)
            mm1_block(2)
            mm2_block(1)
            mm1_block(3)
            mm2_block(2)
            mm2_block(3)

    def p6_scalar(e):
        for j in range(KC):
            wait(e, "gD", G6c(0, j))
            tick("sS", e.copy(WF[0][:, j, :], WL6[:, j % 2, :]))
        for he in range(NHE):
            ex, half = he // 2, he % 2
            for t in range(NTF):
                for m in range(KC):
                    wait(e, "sT", T6_mm1(he, t, m))
                    q = (he * NTF + t) * 8 + m
                    tick("sS", e.activation(HSB[:, t % 2, m, :],
                                            ps_h[q % 2][:, :], AF.Gelu,
                                            bias=b1c[:, half * 8 + m, ex:ex + 1],
                                            scale=1.0))
                    if t == 1 and he + 1 < NHE:
                        wait(e, "gD", G6c(he + 1, m))
                        tick("sS", e.copy(WF[(he + 1) % 2][:, m, :],
                                          WL6[:, m % 2, :]))

    def p6_vector(e):
        for j in range(KC):
            wait(e, "gD", G6c(0, 8 + j))
            tick("sV", e.tensor_copy(WF[0][:, 8 + j, :], WL6[:, j % 2, :]))
        for he in range(NHE):
            for t in range(NTF):
                for ps in range(2):
                    for j in range(4):
                        wait(e, "sT", T6_mm2(he, t, ps, j))
                        d = ps * 4 + j
                        tick("sV", e.tensor_add(A[:, d, t * TF:(t + 1) * TF],
                                                ps_y[:, j, :],
                                                A[:, d, t * TF:(t + 1) * TF]))
                        if t == 2 and he + 1 < NHE:
                            a = ps * 4 + j
                            wait(e, "gD", G6c(he + 1, 8 + a))
                            tick("sV", e.tensor_copy(
                                WF[(he + 1) % 2][:, 8 + a, :],
                                WL6[:, a % 2, :]))

    blk.gpsimd(p6_gpsimd)
    blk.tensor(p6_tensor)
    blk.scalar(p6_scalar)
    blk.vector(p6_vector)
    p6.close()

    if debug:
        dump([("afin", A[:, :, :])])
    if stop_after == "P6":
        return finish()

    p7 = ExitStack()
    OUT5 = [sb_in(p7, f"OUT5_{i}", [128, D], F32) for i in range(2)]
    ST7 = sb_in(p7, "ST7", [128, 10], F32)
    GG = sb_in(p7, "GG", [128, LC], F32)
    ps_7 = [psum_in(p7, f"ps_7{i}", [128, D]) for i in range(2)]
    ps_sq7 = psum_in(p7, "ps_sq7", [128, 512])

    P7S = dict(C)

    def T7(c):
        return P7S["sT"] + c + 1

    def V7pre_end():
        return P7S["sV"] + KC + 1

    def V7_var(c):
        return P7S["sV"] + KC + 1 + 12 * c + 8

    def V7(c):
        return P7S["sV"] + KC + 1 + 12 * (c + 1)

    def S7_sq(c):
        return P7S["sS"] + 3 * c + 2

    def S7_sd(c):
        return P7S["sS"] + 3 * c + 3

    def D7sem(c):
        return "sD" if c % 2 == 0 else "sE"

    def D7(c):
        return P7S[D7sem(c)] + 16 * (c // 2 + 1)

    def p7_vector_pre(e):
        vsp(e)
        for k in range(KC):
            tick("sV", e.tensor_scalar(A[:, k, :], A[:, k, :],
                                       sb2c[:, k:k + 1], None, ALU.add))
        tick("sV", e.tensor_mul(GG[:, :], NG[:, :], NG[:, :]))
    blk.vector(p7_vector_pre)

    def p7_tensor(e):
        wait(e, "sV", V7pre_end())
        for c in range(LC):
            if c >= 2:
                wait(e, "sV", V7(c - 2))
            for k in range(KC):
                instr = e.transpose(ps_7[c % 2][:, k * 128:(k + 1) * 128],
                                    A[:, k, c * 128:(c + 1) * 128],
                                    ident_f[:, :])
                if k == KC - 1:
                    tick("sT", instr)
    blk.tensor(p7_tensor)

    def p7_scalar(e):
        for c in range(LC):
            wait(e, "sT", T7(c))
            if c >= 1:
                wait(e, "sV", V7_var(c - 1))
            tick("sS", e.activation(ps_sq7[:, :], ps_7[c % 2][:, 0:512],
                                    AF.Square, bias=0.0, scale=1.0,
                                    accum_out=ST7[:, 6:7]))
            wait(e, "sS")
            tick("sS", e.activation(ps_sq7[:, :], ps_7[c % 2][:, 512:1024],
                                    AF.Square, bias=0.0, scale=1.0,
                                    accum_out=ST7[:, 7:8]))
            wait(e, "sV", V7_var(c))
            tick("sS", e.activation(ST7[:, 4:5], ST7[:, 3:4], AF.Sqrt,
                                    bias=eps_col[:, :], scale=1.0))
    blk.scalar(p7_scalar)

    def p7_vector(e):
        for c in range(LC):
            wait(e, "sT", T7(c))
            tick("sV", e.reduce_sum(ST7[:, 0:1], ps_7[c % 2][:, :], axis=AX.X))
            wait(e, "sS", S7_sq(c))
            tick("sV", e.tensor_add(ST7[:, 1:2], ST7[:, 6:7], ST7[:, 7:8]))
            vsp(e)
            tick("sV", e.tensor_mul(ST7[:, 0:1], ST7[:, 0:1], NG[:, c:c + 1]))
            vsp(e)
            tick("sV", e.tensor_mul(ST7[:, 1:2], ST7[:, 1:2], GG[:, c:c + 1]))
            vsp(e)
            tick("sV", e.tensor_single_scalar(ST7[:, 0:1], ST7[:, 0:1], 1.0 / D,
                                              ALU.mult))
            vsp(e)
            tick("sV", e.tensor_single_scalar(ST7[:, 1:2], ST7[:, 1:2], 1.0 / D,
                                              ALU.mult))
            vsp(e)
            tick("sV", e.tensor_mul(ST7[:, 2:3], ST7[:, 0:1], ST7[:, 0:1]))
            vsp(e)
            tick("sV", e.tensor_sub(ST7[:, 3:4], ST7[:, 1:2], ST7[:, 2:3]))
            wait(e, "sS", S7_sd(c))
            tick("sV", e.reciprocal(ST7[:, 5:6], ST7[:, 4:5]))
            vsp(e)
            tick("sV", e.tensor_mul(ST7[:, 8:9], NG[:, c:c + 1], ST7[:, 5:6]))
            vsp(e)
            tick("sV", e.tensor_mul(ST7[:, 9:10], ST7[:, 0:1], ST7[:, 5:6]))
            vsp(e)
            if c >= 2:
                wait(e, D7sem(c - 2), D7(c - 2))
            tick("sV", e.tensor_scalar(OUT5[c % 2][:, :], ps_7[c % 2][:, :],
                                       ST7[:, 8:9], ST7[:, 9:10],
                                       ALU.mult, ALU.subtract))
    blk.vector(p7_vector)

    def p7_sync(e):
        for c in range(LC):
            wait(e, "sV", V7(c))
            tick(D7sem(c), e.dma_start(
                out_d.rearrange("(c p) n -> p c n", p=128)[:, c, :],
                OUT5[c % 2][:, :]), 16)
        wait(e, "sD")
        wait(e, "sE")
    blk.sync(p7_sync)
    p7.close()

    return finish()


def _shard_inputs(inputs):
    x = np.ascontiguousarray(np.asarray(inputs["L0_attn_x"], np.float32))
    names = {
        "W_q": "L0_attn_W_q", "b_q": "L0_attn_b_q",
        "W_k": "L0_attn_W_k", "b_k": "L0_attn_b_k",
        "W_v": "L0_attn_W_v", "b_v": "L0_attn_b_v",
        "W_o": "L0_attn_W_o", "b_o": "L0_attn_b_o",
        "gate_w": "L0_ffn_gate_w", "gate_b": "L0_ffn_gate_b",
        "rms_w": "L0_ffn_rms_w", "w1": "L0_ffn_w1", "b1": "L0_ffn_b1",
        "w2": "L0_ffn_w2", "b2": "L0_ffn_b2",
    }
    base = {k: np.ascontiguousarray(np.asarray(inputs[v], np.float32))
            for k, v in names.items()}
    base["ksi"] = np.ascontiguousarray(
        np.asarray(inputs["L0_attn_k_sample_idx"], np.int64).astype(np.float32))
    in_maps = []
    for c in range(8):
        m = dict(base)
        m["x"] = np.ascontiguousarray(x[c])
        in_maps.append(m)
    return in_maps


def kernel(**inputs):
    in_maps = _shard_inputs(inputs)
    nc = build_nc()
    res = run_bass_kernel_spmd(nc, in_maps, core_ids=list(range(8)))
    out = np.stack([np.asarray(r["out"], np.float32) for r in res.results],
                   axis=0)
    return out
